# version 04
# baseline (speedup 1.0000x reference)
import numpy as np
import ml_dtypes
import concourse.bass as bass
import concourse.mybir as mybir
from concourse.bass_utils import run_bass_kernel_spmd

F32 = mybir.dt.float32
BF16 = mybir.dt.bfloat16
AF = mybir.ActivationFunctionType
OP = mybir.AluOpType
AX = mybir.AxisListType
BFNP = ml_dtypes.bfloat16

B, S, DM = 4, 4096, 1024
TOK = 2048
NCORES = 8
HA, DA = 16, 64
SCALE_A = float(DA) ** -0.25
SCALE_B = 1.0 / 8.0
LN_EPS = 1e-5
DEBUG_B = False


class Sem:
    def __init__(self, ctx, nc, name):
        self.h = ctx.enter_context(nc.semaphore(name))
        self.v = 0

    def inc(self, instr, n=1):
        instr.then_inc(self.h, n)
        self.v += n
        return instr

    def wait(self, eng, v=None):
        eng.wait_ge(self.h, self.v if v is None else v)


class Alloc:
    def __init__(self, ctx, nc, name, n_elems, dtype):
        self.t = ctx.enter_context(nc.sbuf_tensor(name, [128, n_elems], dtype))
        self.n = n_elems
        self.off = 0

    def tile(self, w):
        assert self.off + w <= self.n, f"sbuf overflow {self.off}+{w}>{self.n}"
        ap = self.t[:, self.off:self.off + w]
        self.off += w
        return ap

    def mark(self):
        return self.off

    def reset(self, m):
        self.off = m


class PsumPool:
    def __init__(self, ctx, nc, s_ev, n=8):
        self.n = n
        self.banks = [ctx.enter_context(nc.psum_tensor(f"psb{i}", [128, 512], F32))
                      for i in range(n)]
        self.guard = [0] * n
        self.rr = 0
        self.s_ev = s_ev

    def acquire(self, te, nbank=1):
        out = []
        for _ in range(nbank):
            k = self.rr % self.n
            self.rr += 1
            if self.guard[k] > 0:
                te.wait_ge(self.s_ev.h, self.guard[k])
            out.append((k, self.banks[k]))
        return out if nbank > 1 else out[0]

    def release(self, k):
        self.guard[k] = self.s_ev.v


class BankGuard:
    def __init__(self, ctx, nc, name, shape, dtype, s_ev):
        self.t = ctx.enter_context(nc.psum_tensor(name, shape, dtype))
        self.guard = 0
        self.s_ev = s_ev

    def acquire(self, te):
        if self.guard > 0:
            te.wait_ge(self.s_ev.h, self.guard)
        return self.t

    def release(self):
        self.guard = self.s_ev.v


class Ctx:
    def __init__(self, ctx, nc, f32_elems, bf16_elems, n_psum=8):
        self.nc = nc
        self.s_ld = Sem(ctx, nc, "s_ld")
        self.s_st = Sem(ctx, nc, "s_st")
        self.s_mm = Sem(ctx, nc, "s_mm")
        self.s_ev = Sem(ctx, nc, "s_ev")
        self.s_sc = Sem(ctx, nc, "s_sc")
        self.f32 = Alloc(ctx, nc, "ar_f32", f32_elems, F32)
        self.bf = Alloc(ctx, nc, "ar_bf", bf16_elems, BF16)
        self.pp = PsumPool(ctx, nc, self.s_ev, n_psum)
        self.ones_col_bf = self.bf.tile(1)
        self.ones_row_f = self.f32.tile(128)
        self.ones_row_bf = self.bf.tile(128)
        self.ones_all_f = self.f32.tile(64)
        with nc.Block() as blk:
            @blk.vector
            def _(e):
                e.memset(self.ones_col_bf[:, :], 1.0)
                e.memset(self.ones_row_f[0:1, :], 1.0)
                e.memset(self.ones_row_bf[0:1, :], 1.0)
                e.memset(self.ones_all_f[:, :], 1.0)


def emit_ln(cx, xbf, xn, T):
    nc = cx.nc
    tw = min(T, 512)
    mf, mb = cx.f32.mark(), cx.bf.mark()
    sq8 = cx.bf.tile(8 * tw).rearrange("p (c t) -> p c t", c=8)
    stats = cx.f32.tile(2 * T)
    arow = cx.f32.tile(T)
    brow = cx.f32.tile(T)
    abc = cx.f32.tile(tw)
    bbc = cx.f32.tile(tw)
    ntc = T // tw

    for tc in range(ntc):
        lo = tc * tw
        with nc.Block() as blk:
            bk0 = bk1 = None
            for dc in range(8):
                @blk.scalar
                def _(e, dc=dc):
                    cx.s_sc.inc(e.activation(sq8[:, dc, :], xbf[:, dc, lo:lo + tw],
                                             AF.Square), 1)

                @blk.tensor
                def _(e, dc=dc):
                    nonlocal bk0, bk1
                    if dc == 0:
                        bk0 = cx.pp.acquire(e)
                        bk1 = cx.pp.acquire(e)
                    cx.s_sc.wait(e)
                    e.matmul(bk0[1][0:1, 0:tw], cx.ones_col_bf[:, :], xbf[:, dc, lo:lo + tw],
                             start=(dc == 0), stop=(dc == 7))
                    mm = e.matmul(bk1[1][0:1, 0:tw], cx.ones_col_bf[:, :], sq8[:, dc, :],
                                  start=(dc == 0), stop=(dc == 7))
                    if dc == 7:
                        cx.s_mm.inc(mm, 1)

            @blk.vector
            def _(e):
                cx.s_mm.wait(e)
                cx.s_ev.inc(e.tensor_scalar(stats[0:1, lo:lo + tw], bk0[1][0:1, 0:tw],
                                            1.0 / 1024.0, None, op0=OP.mult), 1)
                cx.pp.release(bk0[0])
                cx.s_ev.inc(e.tensor_scalar(stats[0:1, T + lo:T + lo + tw], bk1[1][0:1, 0:tw],
                                            1.0 / 1024.0, None, op0=OP.mult), 1)
                cx.pp.release(bk1[0])

    with nc.Block() as blk:
        @blk.vector
        def _(e):
            mu = stats[0:1, 0:T]
            m2 = stats[0:1, T:2 * T]
            e.tensor_tensor(brow[0:1, :], mu, mu, op=OP.mult)
            e.tensor_tensor(arow[0:1, :], m2, brow[0:1, :], op=OP.subtract)
            e.tensor_scalar(brow[0:1, :], arow[0:1, :], LN_EPS, None, op0=OP.add)
            cx.s_ev.inc(e.reciprocal(arow[0:1, :], brow[0:1, :]), 1)
    with nc.Block() as blk:
        @blk.scalar
        def _(e):
            cx.s_sc.inc(e.activation(arow[0:1, :], arow[0:1, :], AF.Sqrt), 1)
    with nc.Block() as blk:
        @blk.vector
        def _(e):
            mu = stats[0:1, 0:T]
            e.tensor_tensor(brow[0:1, :], mu, arow[0:1, :], op=OP.mult)
            e.tensor_scalar(brow[0:1, :], brow[0:1, :], -1.0, None, op0=OP.mult)

    for tc in range(ntc):
        lo = tc * tw
        with nc.Block() as blk:
            bka = bkb = None

            @blk.tensor
            def _(e):
                nonlocal bka, bkb
                bka = cx.pp.acquire(e)
                bkb = cx.pp.acquire(e)
                e.matmul(bka[1][:, 0:tw], cx.ones_row_f[0:1, :], arow[0:1, lo:lo + tw],
                         start=True, stop=True)
                cx.s_mm.inc(e.matmul(bkb[1][:, 0:tw], cx.ones_row_f[0:1, :], brow[0:1, lo:lo + tw],
                                     start=True, stop=True), 1)

            @blk.vector
            def _(e):
                cx.s_mm.wait(e)
                e.tensor_copy(abc[:, :], bka[1][:, 0:tw])
                e.tensor_copy(bbc[:, :], bkb[1][:, 0:tw])
                for dc in range(8):
                    e.tensor_tensor(xn[:, dc, lo:lo + tw], xbf[:, dc, lo:lo + tw],
                                    abc[:, :], op=OP.mult)
                    last = e.tensor_tensor(xn[:, dc, lo:lo + tw], xn[:, dc, lo:lo + tw],
                                           bbc[:, :], op=OP.add)
                cx.s_ev.inc(last, 1)
                cx.pp.release(bka[0])
                cx.pp.release(bkb[0])

    cx.f32.reset(mf)
    cx.bf.reset(mb)


def emit_proj_fm(cx, w_bf, xn, out_hbm, T, scale, bias_f32, stage):
    nc = cx.nc
    ntc = T // 512
    sidx = 0
    for do in range(8):
        for tc in range(ntc):
            lo = tc * 512
            with nc.Block() as blk:
                bk = None

                @blk.tensor
                def _(e):
                    nonlocal bk
                    bk = cx.pp.acquire(e)
                    for ci in range(8):
                        mm = e.matmul(bk[1][:, :], w_bf[:, ci, do * 128:do * 128 + 128],
                                      xn[:, ci, lo:lo + 512],
                                      start=(ci == 0), stop=(ci == 7))
                    cx.s_mm.inc(mm, 1)

                @blk.vector
                def _(e):
                    cx.s_mm.wait(e)
                    sl = sidx % 2
                    if sidx >= 2:
                        e.wait_ge(cx.s_st.h, cx.s_st.v - 16)
                    if bias_f32 is not None:
                        op = e.tensor_scalar(stage[:, sl, :], bk[1][:, :],
                                             bias_f32[:, do:do + 1], scale,
                                             op0=OP.add, op1=OP.mult)
                    else:
                        op = e.tensor_scalar(stage[:, sl, :], bk[1][:, :],
                                             scale, None, op0=OP.mult)
                    cx.s_ev.inc(op, 1)
                    cx.pp.release(bk[0])

                @blk.sync
                def _(e):
                    cx.s_ev.wait(e)
                    cx.s_st.inc(e.dma_start(out_hbm[do, :, lo:lo + 512],
                                            stage[:, sidx % 2, :]), 16)
            sidx += 1


def build_launch_A():
    import contextlib
    es = contextlib.ExitStack()
    nc = bass.Bass("TRN2", target_bir_lowering=False)
    xt = nc.dram_tensor("xt", [8, 128, TOK], F32, kind="ExternalInput").ap()
    wq = nc.dram_tensor("wq", [8, 128, 1024], F32, kind="ExternalInput").ap()
    wk = nc.dram_tensor("wk", [8, 128, 1024], F32, kind="ExternalInput").ap()
    wv = nc.dram_tensor("wv", [8, 128, 1024], F32, kind="ExternalInput").ap()
    bq = nc.dram_tensor("bq", [128, 8], F32, kind="ExternalInput").ap()
    bk_ = nc.dram_tensor("bk", [128, 8], F32, kind="ExternalInput").ap()
    bv = nc.dram_tensor("bv", [1, 1024], F32, kind="ExternalInput").ap()
    qa = nc.dram_tensor("qa", [8, 128, TOK], BF16, kind="ExternalOutput").ap()
    ka = nc.dram_tensor("ka", [8, 128, TOK], BF16, kind="ExternalOutput").ap()
    va = nc.dram_tensor("va", [16, 128, 1024], BF16, kind="ExternalOutput").ap()
    xl = nc.dram_tensor("xl", [128, 256], F32, kind="ExternalOutput").ap()

    with es:
        cx = Ctx(es, nc, f32_elems=6 * TOK + 2048, bf16_elems=2 * 8 * TOK + 8 * 1024 * 3 + 8192)
        xbf_t = cx.bf.tile(8 * TOK)
        xbf = xbf_t.rearrange("p (c t) -> p c t", c=8)
        xn_t = cx.bf.tile(8 * TOK)
        xn = xn_t.rearrange("p (c t) -> p c t", c=8)
        wqb = cx.bf.tile(8 * 1024).rearrange("p (c d) -> p c d", c=8)
        wkb = cx.bf.tile(8 * 1024).rearrange("p (c d) -> p c d", c=8)
        wvb = cx.bf.tile(8 * 1024).rearrange("p (c d) -> p c d", c=8)
        stage = cx.bf.tile(3 * 512).rearrange("p (s t) -> p s t", s=3)[:, 0:2, :]
        stage2 = cx.bf.tile(2 * 512).rearrange("p (s t) -> p s t", s=2)
        bvrow = cx.bf.tile(1024)
        bqs = cx.f32.tile(8)
        bks = cx.f32.tile(8)
        xls = cx.f32.tile(256)
        xls3 = xls.rearrange("p (c l) -> p c l", c=8)

        with nc.Block() as blk:
            @blk.gpsimd
            def _(e):
                cx.s_ld.inc(e.dma_start(xbf, xt.rearrange("c p t -> p c t")), 16)
                cx.s_ld.inc(e.dma_start(wqb, wq.rearrange("c p d -> p c d")), 16)
                cx.s_ld.inc(e.dma_start(wkb, wk.rearrange("c p d -> p c d")), 16)
                cx.s_ld.inc(e.dma_start(wvb, wv.rearrange("c p d -> p c d")), 16)
                cx.s_ld.inc(e.dma_start(bvrow[0:1, :], bv[:, :]), 16)
            @blk.sync
            def _(e):
                cx.s_ld.inc(e.dma_start(bqs[:, :], bq[:, :]), 16)
                cx.s_ld.inc(e.dma_start(bks[:, :], bk_[:, :]), 16)
                cx.s_ld.wait(e)
            @blk.vector
            def _(e):
                cx.s_ld.wait(e)
            @blk.scalar
            def _(e):
                cx.s_ld.wait(e)
            @blk.tensor
            def _(e):
                cx.s_ld.wait(e)

        emit_ln(cx, xbf, xn, TOK)

        with nc.Block() as blk:
            @blk.vector
            def _(e):
                for dc in range(8):
                    e.reduce_sum(xls3[:, dc, :],
                                 xn[:, dc, :].rearrange("p (l w) -> p l w", w=64),
                                 axis=AX.X)
                e.tensor_scalar(xls[:, :], xls[:, :], 1.0 / 64.0, None, op0=OP.mult)
            @blk.sync
            def _(e):
                pass
        with nc.Block() as blk:
            @blk.sync
            def _(e):
                cx.s_st.inc(e.dma_start(xl[:, :], xls[:, :]), 16)

        emit_proj_fm(cx, wqb, xn, qa, TOK, SCALE_A, bqs, stage)
        emit_proj_fm(cx, wkb, xn, ka, TOK, SCALE_A, bks, stage)

        sidx = 0
        for tt in range(16):
            for eo in range(2):
                lo = eo * 512
                with nc.Block() as blk:
                    bk2 = None

                    @blk.tensor
                    def _(e):
                        nonlocal bk2
                        bk2 = cx.pp.acquire(e)
                        for ci in range(8):
                            e.matmul(bk2[1][:, :], xn[:, ci, tt * 128:tt * 128 + 128],
                                     wvb[:, ci, lo:lo + 512],
                                     start=(ci == 0), stop=False)
                        cx.s_mm.inc(e.matmul(bk2[1][:, :], cx.ones_row_bf[0:1, 0:128],
                                             bvrow[0:1, lo:lo + 512],
                                             start=False, stop=True), 1)

                    @blk.vector
                    def _(e):
                        cx.s_mm.wait(e)
                        if sidx >= 2:
                            e.wait_ge(cx.s_st.h, cx.s_st.v - 16)
                        cx.s_ev.inc(e.tensor_copy(stage2[:, sidx % 2, :], bk2[1][:, :]), 1)
                        cx.pp.release(bk2[0])

                    @blk.sync
                    def _(e):
                        cx.s_ev.wait(e)
                        cx.s_st.inc(e.dma_start(va[tt, :, lo:lo + 512],
                                                stage2[:, sidx % 2, :]), 16)
                sidx += 1

        with nc.Block() as blk:
            @blk.sync
            def _(e):
                cx.s_st.wait(e)
    return nc


def build_launch_B():
    import contextlib
    es = contextlib.ExitStack()
    nc = bass.Bass("TRN2", target_bir_lowering=False)
    qa = nc.dram_tensor("qa", [128, 4, 4096], BF16, kind="ExternalInput").ap()
    ka = nc.dram_tensor("ka", [128, 4, 4096], BF16, kind="ExternalInput").ap()
    va = nc.dram_tensor("va", [128, 4, 32, 128], BF16, kind="ExternalInput").ap()
    xlg = nc.dram_tensor("xlg", [128, 8, 4, 64], F32, kind="ExternalInput").ap()
    wqm = nc.dram_tensor("wqm", [128, 8, 128], F32, kind="ExternalInput").ap()
    wkm = nc.dram_tensor("wkm", [128, 8, 128], F32, kind="ExternalInput").ap()
    bqm = nc.dram_tensor("bqm", [128, 1], F32, kind="ExternalInput").ap()
    bkm = nc.dram_tensor("bkm", [128, 1], F32, kind="ExternalInput").ap()
    tb = nc.dram_tensor("tb", [128, 2, 3, 128], F32, kind="ExternalInput").ap()
    ic = nc.dram_tensor("iconsts", [128, 4, 128], F32, kind="ExternalInput").ap()
    ctxo = nc.dram_tensor("ctxo", [128, 4, 4096], BF16, kind="ExternalOutput").ap()
    dbg = None
    if DEBUG_B:
        dbg = nc.dram_tensor("dbg", [128, 16, 512], F32, kind="ExternalOutput").ap()

    with es:
        cx = Ctx(es, nc, f32_elems=3072,
                 bf16_elems=3 * 4 * 4096 + 4096 + 2 * 2048 + 8 * 512 + 16 * 128 + 4096,
                 n_psum=7)
        tp = BankGuard(es, nc, "tp_bf", [128, 1024], BF16, cx.s_ev)
        qam = cx.bf.tile(4 * 4096).rearrange("p (b s) -> p b s", b=4)
        kam = cx.bf.tile(4 * 4096).rearrange("p (b s) -> p b s", b=4)
        vam = cx.bf.tile(4 * 4096).rearrange("p (b t d) -> p b t d", b=4, t=32)
        E1 = cx.bf.tile(4096)
        E3h = [cx.bf.tile(2048).rearrange("p (t l) -> p t l", t=32) for _ in range(2)]
        xls = cx.bf.tile(8 * 4 * 64).rearrange("p (c b l) -> p c b l", c=8, b=4)
        wqs = cx.bf.tile(8 * 128).rearrange("p (c d) -> p c d", c=8)
        wks = cx.bf.tile(8 * 128).rearrange("p (c d) -> p c d", c=8)
        tbs = cx.bf.tile(2 * 3 * 128).rearrange("p (h r f) -> p h r f", h=2, r=3)
        ident = cx.bf.tile(128)
        ipair = cx.bf.tile(3 * 128).rearrange("p (i l) -> p i l", i=3)
        stage = cx.bf.tile(2 * 512).rearrange("p (s t) -> p s t", s=2)
        w = {}
        for nm in ["qlp", "klp", "e2t", "K2p"]:
            w[nm] = cx.bf.tile(64)
        for nm in ["K2Tw", "Vw", "VwT", "Gw", "GwT", "t1w", "tTw", "t3w", "t5w",
                   "Vnw", "k3vw", "den3sb", "M2pad"]:
            w[nm] = cx.bf.tile(128)
        ones_f = cx.f32.tile(64)
        w["rmx"] = cx.f32.tile(2)
        w["rmxr"] = cx.f32.tile(2)
        w["r3"] = cx.f32.tile(2)
        w["r3r"] = cx.f32.tile(2)
        w["denrow2"] = cx.f32.tile(512)
        w["d2"] = cx.f32.tile(1)
        w["r2"] = cx.f32.tile(1)
        w["denrow"] = cx.f32.tile(512)
        w["ones_f"] = ones_f
        w["ident"] = ident
        w["ipair"] = ipair
        w["tp"] = tp
        w["E3h_l"] = E3h
        w["E1"] = E1

        with nc.Block() as blk:
            @blk.gpsimd
            def _(e):
                cx.s_ld.inc(e.dma_start(xls, xlg.rearrange("p c b l -> p c b l")), 16)
                cx.s_ld.inc(e.dma_start(wqs, wqm.rearrange("p c d -> p c d")), 16)
                cx.s_ld.inc(e.dma_start(wks, wkm.rearrange("p c d -> p c d")), 16)
                cx.s_ld.inc(e.dma_start(tbs, tb.rearrange("p h r f -> p h r f")), 16)
                cx.s_ld.inc(e.dma_start(ident[:, :], ic[:, 0, :]), 16)
                cx.s_ld.inc(e.dma_start(ipair.rearrange("p i l -> p i l"),
                                        ic[:, 1:4, :]), 16)
                cx.s_ld.inc(e.dma_start(w["rmx"][:, 0:1], bqm[:, :]), 16)
                cx.s_ld.inc(e.dma_start(w["rmx"][:, 1:2], bkm[:, :]), 16)
            @blk.sync
            def _(e):
                cx.s_ld.inc(e.dma_start(qam, qa.rearrange("p b s -> p b s")), 16)
                cx.s_ld.inc(e.dma_start(kam, ka.rearrange("p b s -> p b s")), 16)
                cx.s_ld.inc(e.dma_start(vam, va.rearrange("p b t d -> p b t d")), 16)
                cx.s_ld.wait(e)
            @blk.vector
            def _(e):
                cx.s_ld.wait(e)
                e.memset(ones_f[:, :], 1.0)
                e.memset(w["M2pad"][:, :], 0.0)
            @blk.tensor
            def _(e):
                cx.s_ld.wait(e)
            @blk.scalar
            def _(e):
                cx.s_ld.wait(e)
            @blk.gpsimd
            def _(e):
                cx.s_ld.wait(e)
        bq_t = cx.f32.tile(1)
        bk_t = cx.f32.tile(1)
        with nc.Block() as blk:
            @blk.vector
            def _(e):
                e.tensor_copy(bq_t[:, :], w["rmx"][:, 0:1])
                e.tensor_copy(bk_t[:, :], w["rmx"][:, 1:2])
        w["bq"] = bq_t
        w["bk"] = bk_t

        w["dbg"] = dbg
        for b in range(4):
            emit_attn_a_batch(cx, b, w, qam, kam, vam, xls, wqs, wks, tbs, stage, ctxo)

        with nc.Block() as blk:
            @blk.sync
            def _(e):
                cx.s_st.wait(e)
    return nc


def emit_wide_product(cx, lhsTw, rhsw, out_sb=None, stt_const=None, stt_out=None,
                      scale=None):
    nc = cx.nc
    with nc.Block() as blk:
        bk = [None]

        @blk.tensor
        def _(e):
            bk[0] = cx.pp.acquire(e)
            for h in range(2):
                mm = e.matmul(bk[0][1][0:64, 64 * h:64 * h + 64],
                              lhsTw[0:64, 64 * h:64 * h + 64],
                              rhsw[0:64, 64 * h:64 * h + 64],
                              start=True, stop=True)
            cx.s_mm.inc(mm, 1)

        @blk.vector
        def _(e):
            cx.s_mm.wait(e)
            if stt_out is not None:
                op = e.scalar_tensor_tensor(stt_out[0:64, :], bk[0][1][0:64, 0:128],
                                            -1.0, stt_const[0:64, :],
                                            op0=OP.mult, op1=OP.add)
            elif scale is not None:
                op = e.tensor_scalar(out_sb[0:64, :], bk[0][1][0:64, 0:128], scale,
                                     None, op0=OP.mult)
            else:
                op = e.tensor_copy(out_sb[0:64, :], bk[0][1][0:64, 0:128])
            cx.s_ev.inc(op, 1)
            cx.pp.release(bk[0][0])


def emit_transpose_wide(cx, w, src, dst):
    nc = cx.nc
    tp = w["tp"]
    with nc.Block() as blk:
        @blk.tensor
        def _(e):
            pb = tp.acquire(e)
            e.transpose(pb[0:64, 0:64], src[0:64, 0:64], w["ident"][0:64, 0:64])
            cx.s_mm.inc(e.transpose(pb[0:64, 64:128], src[0:64, 64:128],
                                    w["ident"][0:64, 0:64]), 1)

        @blk.vector
        def _(e):
            cx.s_mm.wait(e)
            cx.s_ev.inc(e.tensor_copy(dst[0:64, 0:128], tp.t[0:64, 0:128]), 1)
            tp.release()


def emit_dbg(cx, w, slot, src, pad2=False):
    if w.get("dbg") is None:
        return
    nc = cx.nc
    with nc.Block() as blk:
        @blk.gpsimd
        def _(e):
            n = src.shape[-1] if len(src.shape) == 2 else 512
            with nc.allow_non_contiguous_dma(reason="debug"):
                cx.s_st.inc(e.dma_start(w["dbg"][0:src.shape[0], slot, 0:n], src), 16)


def emit_attn_a_batch(cx, b, w, qam, kam, vam, xls, wqs, wks, tbs, stage, ctxo):
    nc = cx.nc
    with nc.Block() as blk:
        bkq = [None]
        bkk = [None]

        @blk.tensor
        def _(e):
            bkq[0] = cx.pp.acquire(e)
            bkk[0] = cx.pp.acquire(e)
            for ci in range(8):
                e.matmul(bkq[0][1][:, 0:64], wqs[:, ci, :], xls[:, ci, b, :],
                         start=(ci == 0), stop=(ci == 7))
                mm = e.matmul(bkk[0][1][:, 0:64], wks[:, ci, :], xls[:, ci, b, :],
                              start=(ci == 0), stop=(ci == 7))
            cx.s_mm.inc(mm, 1)

        @blk.vector
        def _(e):
            cx.s_mm.wait(e)
            cx.s_ev.inc(e.tensor_scalar(w["qlp"][:, :], bkq[0][1][:, 0:64],
                                        w["bq"][:, 0:1], SCALE_A,
                                        op0=OP.add, op1=OP.mult), 1)
            cx.pp.release(bkq[0][0])
            cx.s_ev.inc(e.tensor_scalar(w["klp"][:, :], bkk[0][1][:, 0:64],
                                        w["bk"][:, 0:1], SCALE_A,
                                        op0=OP.add, op1=OP.mult), 1)
            cx.pp.release(bkk[0][0])

    if b == 0:
        emit_dbg(cx, w, 0, w["qlp"][:, 0:64])
        emit_dbg(cx, w, 1, w["klp"][:, 0:64])
    with nc.Block() as blk:
        bk2 = [None]

        @blk.tensor
        def _(e):
            bk2[0] = cx.pp.acquire(e)
            e.matmul(bk2[0][1][0:64, 0:64], w["qlp"][0:64, :], w["klp"][0:64, :],
                     start=True, stop=True, tile_position=(0, 0))
            cx.s_mm.inc(e.matmul(bk2[0][1][64:128, 0:64], w["qlp"][64:128, :],
                                 w["klp"][64:128, :], start=True, stop=True,
                                 tile_position=(64, 64)), 1)

        @blk.scalar
        def _(e):
            cx.s_mm.wait(e)
            cx.s_sc.inc(e.activation(w["e2t"][:, :], bk2[0][1][:, 0:64], AF.Exp,
                                     accum_out=w["d2"][:, 0:1]), 1)

        @blk.vector
        def _(e):
            cx.s_sc.wait(e)
            cx.s_ev.inc(e.reciprocal(w["r2"][:, 0:1], w["d2"][:, 0:1]), 1)
            cx.s_ev.wait(e)
            cx.s_ev.inc(e.tensor_scalar(w["K2p"][:, :], w["e2t"][:, :],
                                        w["r2"][:, 0:1], None, op0=OP.mult), 1)
            cx.pp.release(bk2[0][0])

    with nc.Block() as blk:
        @blk.tensor
        def _(e):
            pb = w["tp"].acquire(e)
            cx.s_mm.inc(e.transpose(pb[0:64, 0:128], w["K2p"][:, :], w["ident"][:, :]), 1)

        @blk.vector
        def _(e):
            cx.s_mm.wait(e)
            cx.s_ev.inc(e.tensor_copy(w["K2Tw"][0:64, :], w["tp"].t[0:64, 0:128]), 1)
            w["tp"].release()

    if b == 0:
        emit_dbg(cx, w, 2, w["K2p"][:, 0:64])
        emit_dbg(cx, w, 3, w["K2Tw"][0:64, 0:128])
        emit_dbg(cx, w, 14, w["d2"][:, 0:1], pad2=True)
        emit_dbg(cx, w, 15, w["e2t"][:, 0:64])
    with nc.Block() as blk:
        bkc = [None]

        @blk.tensor
        def _(e):
            bkc[0] = cx.pp.acquire(e)
            e.matmul(bkc[0][1][0:1, 0:64], cx.ones_col_bf[0:64, 0:1], w["K2p"][0:64, :],
                     start=True, stop=True, tile_position=(0, 0))
            cx.s_mm.inc(e.matmul(bkc[0][1][32:33, 0:64], cx.ones_col_bf[64:128, 0:1],
                                 w["K2p"][64:128, :], start=True, stop=True,
                                 tile_position=(64, 32)), 1)

        @blk.vector
        def _(e):
            cx.s_mm.wait(e)
            e.tensor_reduce(w["rmx"][0:1, 0:1], bkc[0][1][0:1, 0:64], axis=AX.X, op=OP.max)
            cx.s_ev.inc(e.tensor_reduce(w["rmx"][32:33, 0:1], bkc[0][1][32:33, 0:64],
                                        axis=AX.X, op=OP.max), 1)
            cx.s_ev.wait(e)
            e.reciprocal(w["rmxr"][0:1, 0:1], w["rmx"][0:1, 0:1])
            cx.s_ev.inc(e.reciprocal(w["rmxr"][32:33, 0:1], w["rmx"][32:33, 0:1]), 1)
            cx.pp.release(bkc[0][0])

    with nc.Block() as blk:
        bka = [None]
        bkb = [None]

        @blk.tensor
        def _(e):
            bka[0] = cx.pp.acquire(e)
            bkb[0] = cx.pp.acquire(e)
            e.matmul(bka[0][1][0:64, 0:1], w["ones_f"][0:1, :], w["rmxr"][0:1, 0:1],
                     start=True, stop=True)
            cx.s_mm.inc(e.matmul(bkb[0][1][0:64, 0:1], w["ones_f"][32:33, :],
                                 w["rmxr"][32:33, 0:1], start=True, stop=True,
                                 tile_position=(32, 0)), 1)

        @blk.vector
        def _(e):
            cx.s_mm.wait(e)
            e.tensor_copy(w["rmx"][0:64, 0:1], bka[0][1][0:64, 0:1])
            cx.s_ev.inc(e.tensor_copy(w["rmx"][0:64, 1:2], bkb[0][1][0:64, 0:1]), 1)
            cx.pp.release(bka[0][0])
            cx.pp.release(bkb[0][0])

    with nc.Block() as blk:
        @blk.vector
        def _(e):
            for h in range(2):
                op = e.tensor_scalar(w["Vw"][0:64, 64 * h:64 * h + 64],
                                     w["K2Tw"][0:64, 64 * h:64 * h + 64],
                                     w["rmx"][0:64, h:h + 1], None, op0=OP.mult)
            cx.s_ev.inc(op, 1)
    emit_transpose_wide(cx, w, w["Vw"], w["VwT"])

    for it in range(6):
        emit_wide_product(cx, w["K2Tw"], w["Vw"], out_sb=w["Gw"])
        emit_transpose_wide(cx, w, w["Gw"], w["GwT"])
        with nc.Block() as blk:
            @blk.vector
            def _(e):
                cx.s_ev.inc(e.scalar_tensor_tensor(w["t1w"][0:64, :], w["Gw"][0:64, :],
                                                   -1.0, w["ipair"][0:64, 0, :],
                                                   op0=OP.mult, op1=OP.add), 1)
        emit_wide_product(cx, w["GwT"], w["t1w"], stt_const=w["ipair"][:, 1, :],
                          stt_out=w["t3w"])
        emit_wide_product(cx, w["GwT"], w["t3w"], stt_const=w["ipair"][:, 2, :],
                          stt_out=w["t5w"])
        emit_wide_product(cx, w["VwT"], w["t5w"], out_sb=w["Vnw"], scale=0.25)
        with nc.Block() as blk:
            @blk.vector
            def _(e):
                cx.s_ev.inc(e.tensor_copy(w["Vw"][0:64, :], w["Vnw"][0:64, :]), 1)
        emit_transpose_wide(cx, w, w["Vw"], w["VwT"])

    if b == 0:
        emit_dbg(cx, w, 4, w["Vw"][0:64, 0:128])
        emit_dbg(cx, w, 5, w["VwT"][0:64, 0:128])
        emit_dbg(cx, w, 6, w["rmx"][0:64, 0:2])
    for sg in range(8):
        with nc.Block() as blk:
            bh = [None, None]

            @blk.tensor
            def _(e):
                bh[0] = cx.pp.acquire(e)
                bh[1] = cx.pp.acquire(e)
                for sti in range(4):
                    st = sg * 4 + sti
                    for h in range(2):
                        mm = e.matmul(bh[h][1][:, 64 * sti:64 * sti + 64],
                                      kam[64 * h:64 * h + 64, b, st * 128:st * 128 + 128],
                                      w["qlp"][64 * h:64 * h + 64, :],
                                      start=True, stop=True,
                                      tile_position=(64 * h, 0))
                cx.s_mm.inc(mm, 1)

            @blk.scalar
            def _(e):
                cx.s_mm.wait(e)
                for h in range(2):
                    src = bh[h][1][:, 0:256].rearrange("p (t l) -> p t l", t=4)
                    cx.s_sc.inc(e.activation(E3_dst(w, h, sg), src, AF.Exp), 1)

            @blk.vector
            def _(e):
                cx.s_sc.wait(e)
                cx.s_ev.inc(e.engine_nop(), 1)
                cx.pp.release(bh[0][0])
                cx.pp.release(bh[1][0])

    with nc.Block() as blk:
        bkd = [None]
        bkv = [None]

        @blk.tensor
        def _(e):
            bkd[0] = cx.pp.acquire(e)
            bkv[0] = cx.pp.acquire(e)
            for st in range(32):
                for h in range(2):
                    mm = e.matmul(bkd[0][1][0:1, 64 * h:64 * h + 64],
                                  cx.ones_col_bf[:, 0:1], w["E3h_l"][h][:, st, :],
                                  start=(st == 0), stop=(st == 31))
            cx.s_mm.inc(mm, 1)
            for st in range(32):
                for h in range(2):
                    mm = e.matmul(bkv[0][1][0:64, 64 * h:64 * h + 64],
                                  w["E3h_l"][h][:, st, :],
                                  vam[:, b, st, 64 * h:64 * h + 64],
                                  start=(st == 0), stop=(st == 31))
            cx.s_mm.inc(mm, 1)

        @blk.vector
        def _(e):
            cx.s_mm.wait(e)
            cx.s_ev.inc(e.tensor_copy(w["den3sb"][0:1, 0:128], bkd[0][1][0:1, 0:128]), 1)
            cx.pp.release(bkd[0][0])

        @blk.tensor
        def _(e):
            cx.s_ev.wait(e)
            bkd[0] = cx.pp.acquire(e)
            for h in range(2):
                mm = e.matmul(bkd[0][1][0:64, h:h + 1],
                              w["den3sb"][0:1, 64 * h:64 * h + 64],
                              cx.ones_row_bf[0:1, 0:1], start=True, stop=True)
            cx.s_mm.inc(mm, 1)

        @blk.vector
        def _(e):
            cx.s_mm.wait(e)
            cx.s_ev.inc(e.tensor_copy(w["r3r"][0:64, 0:2], bkd[0][1][0:64, 0:2]), 1)
            cx.s_ev.wait(e)
            e.reciprocal(w["r3"][0:64, 0:1], w["r3r"][0:64, 0:1])
            cx.s_ev.inc(e.reciprocal(w["r3"][0:64, 1:2], w["r3r"][0:64, 1:2]), 1)
            cx.s_ev.wait(e)
            for h in range(2):
                op = e.tensor_scalar(w["k3vw"][0:64, 64 * h:64 * h + 64],
                                     bkv[0][1][0:64, 64 * h:64 * h + 64],
                                     w["r3"][0:64, h:h + 1], None, op0=OP.mult)
            cx.s_ev.inc(op, 1)
            cx.pp.release(bkd[0][0])
            cx.pp.release(bkv[0][0])

    if b == 0:
        emit_dbg(cx, w, 7, w["k3vw"][0:64, 0:128])
        emit_dbg(cx, w, 8, w["den3sb"][0:1, 0:128])
        emit_dbg(cx, w, 9, w["r3r"][0:64, 0:2], pad2=True)
        emit_dbg(cx, w, 15, w["r3"][0:64, 0:2], pad2=True)
    with nc.Block() as blk:
        bkm = [None]

        @blk.tensor
        def _(e):
            bkm[0] = cx.pp.acquire(e)
            for h in range(2):
                mm = e.matmul(bkm[0][1][64 * h:64 * h + 64, 0:64],
                              w["VwT"][0:64, 64 * h:64 * h + 64],
                              w["k3vw"][0:64, 64 * h:64 * h + 64],
                              start=True, stop=True, tile_position=(0, 64 * h))
            cx.s_mm.inc(mm, 1)

        @blk.vector
        def _(e):
            cx.s_mm.wait(e)
            for h in range(2):
                op = e.tensor_copy(w["M2pad"][64 * h:64 * h + 64, 64 * h:64 * h + 64],
                                   bkm[0][1][64 * h:64 * h + 64, 0:64])
            cx.s_ev.inc(op, 1)
            cx.pp.release(bkm[0][0])

    if b == 0:
        emit_dbg(cx, w, 10, w["M2pad"][:, 0:128])
    for sc in range(8):
        with nc.Block() as blk:
            bke = [None]
            bkn = [None]
            bkr = [None]

            @blk.tensor
            def _(e):
                bke[0] = cx.pp.acquire(e)
                e.matmul(bke[0][1][0:64, :], w["klp"][0:64, :],
                         qam[0:64, b, sc * 512:sc * 512 + 512],
                         start=True, stop=True, tile_position=(0, 0))
                cx.s_mm.inc(e.matmul(bke[0][1][64:128, :], w["klp"][64:128, :],
                                     qam[64:128, b, sc * 512:sc * 512 + 512],
                                     start=True, stop=True, tile_position=(64, 64)), 1)

            @blk.scalar
            def _(e):
                cx.s_mm.wait(e)
                cx.s_sc.inc(e.activation(w["E1"][:, sc * 512:sc * 512 + 512],
                                         bke[0][1][:, :], AF.Exp), 1)

            @blk.tensor
            def _(e):
                cx.s_sc.wait(e)
                bkn[0] = cx.pp.acquire(e)
                e.matmul(bkn[0][1][0:1, :], cx.ones_col_bf[0:64, 0:1],
                         w["E1"][0:64, sc * 512:sc * 512 + 512],
                         start=True, stop=True, tile_position=(0, 0))
                cx.s_mm.inc(e.matmul(bkn[0][1][32:33, :], cx.ones_col_bf[64:128, 0:1],
                                     w["E1"][64:128, sc * 512:sc * 512 + 512],
                                     start=True, stop=True, tile_position=(64, 32)), 1)

            @blk.vector
            def _(e):
                cx.s_mm.wait(e)
                e.tensor_copy(w["denrow2"][0:1, :], bkn[0][1][0:1, :])
                cx.s_ev.inc(e.tensor_copy(w["denrow2"][32:33, :], bkn[0][1][32:33, :]), 1)
                cx.s_ev.wait(e)
                e.reciprocal(w["denrow"][0:1, :], w["denrow2"][0:1, :])
                cx.s_ev.inc(e.reciprocal(w["denrow"][32:33, :], w["denrow2"][32:33, :]), 1)
                cx.pp.release(bke[0][0])
                cx.pp.release(bkn[0][0])

            @blk.tensor
            def _(e):
                cx.s_ev.wait(e)
                bkr[0] = cx.pp.acquire(e)
                e.matmul(bkr[0][1][0:64, :], w["ones_f"][0:1, :], w["denrow"][0:1, :],
                         start=True, stop=True, tile_position=(0, 0))
                cx.s_mm.inc(e.matmul(bkr[0][1][64:128, :], w["ones_f"][32:33, :],
                                     w["denrow"][32:33, :], start=True, stop=True,
                                     tile_position=(32, 64)), 1)

            @blk.vector
            def _(e):
                cx.s_mm.wait(e)
                cx.s_ev.inc(e.tensor_tensor(w["E1"][:, sc * 512:sc * 512 + 512],
                                            w["E1"][:, sc * 512:sc * 512 + 512],
                                            bkr[0][1][:, :], op=OP.mult), 1)
                cx.pp.release(bkr[0][0])

    if b == 0:
        emit_dbg(cx, w, 11, w["E1"][:, 0:512])
        emit_dbg(cx, w, 12, w["denrow"][0:1, 0:512])
        emit_dbg(cx, w, 13, w["denrow"][32:33, 0:512])
    for sc in range(8):
        with nc.Block() as blk:
            bkc2 = [None]

            @blk.tensor
            def _(e):
                bkc2[0] = cx.pp.acquire(e)
                ps = bkc2[0][1]
                e.matmul(ps[0:64, :], w["M2pad"][:, 0:64],
                         w["E1"][:, sc * 512:sc * 512 + 512],
                         start=True, stop=False, tile_position=(0, 0))
                e.matmul(ps[64:128, :], w["M2pad"][:, 64:128],
                         w["E1"][:, sc * 512:sc * 512 + 512],
                         start=True, stop=False, tile_position=(0, 64))
                mms = []
                for sti in range(4):
                    st = sc * 4 + sti
                    for h in range(2):
                        co = 64 * h
                        sub = ps[co:co + 64, 128 * sti:128 * sti + 128]
                        if st > 0:
                            mms.append((sub, vam[:, b, st - 1, co:co + 64],
                                        tbs[:, h, 0, :], co))
                        mms.append((sub, vam[:, b, st, co:co + 64],
                                    tbs[:, h, 1, :], co))
                        if st < 31:
                            mms.append((sub, vam[:, b, st + 1, co:co + 64],
                                        tbs[:, h, 2, :], co))
                for i, (sub, lh, rh, co) in enumerate(mms):
                    mm = e.matmul(sub, lh, rh, start=False, stop=(i == len(mms) - 1),
                                  tile_position=(0, co))
                cx.s_mm.inc(mm, 1)

            @blk.vector
            def _(e):
                cx.s_mm.wait(e)
                if b * 8 + sc >= 2:
                    e.wait_ge(cx.s_st.h, cx.s_st.v - 16)
                cx.s_ev.inc(e.tensor_copy(stage[:, (b * 8 + sc) % 2, :],
                                          bkc2[0][1][:, :]), 1)
                cx.pp.release(bkc2[0][0])

            @blk.sync
            def _(e):
                cx.s_ev.wait(e)
                cx.s_st.inc(e.dma_start(ctxo[:, b, sc * 512:sc * 512 + 512],
                                        stage[:, (b * 8 + sc) % 2, :]), 16)


def E3_dst(w, h, sg):
    return w["E3h_l"][h][:, sg * 4:sg * 4 + 4, :]


TBLK = 256
NBLK = TOK // TBLK


def build_launch_C():
    import contextlib
    es = contextlib.ExitStack()
    nc = bass.Bass("TRN2", target_bir_lowering=False)
    xt = nc.dram_tensor("xt", [8, 128, TOK], F32, kind="ExternalInput").ap()
    ctxa = nc.dram_tensor("ctxa", [8, 128, TOK], BF16, kind="ExternalInput").ap()
    wy = nc.dram_tensor("wy", [32, 4, 32, 32], F32, kind="ExternalInput").ap()
    wz = nc.dram_tensor("wz", [128, 32, 128], F32, kind="ExternalInput").ap()
    wcs = nc.dram_tensor("wcs", [128, 128, 1024], F32, kind="ExternalInput").ap()
    wqT = nc.dram_tensor("wqT", [8, 128, 1024], F32, kind="ExternalInput").ap()
    wkT = nc.dram_tensor("wkT", [8, 128, 1024], F32, kind="ExternalInput").ap()
    wvT = nc.dram_tensor("wvT", [8, 128, 1024], F32, kind="ExternalInput").ap()
    xp = nc.dram_tensor("xp", [8, 128, TOK], F32, kind="ExternalOutput").ap()
    qbs = nc.dram_tensor("qbs", [16, 128, 2048], BF16, kind="ExternalOutput").ap()
    kbs = nc.dram_tensor("kbs", [16, 128, 2048], BF16, kind="ExternalOutput").ap()
    k1bs = nc.dram_tensor("k1bs", [16, 128, 2048], BF16, kind="ExternalOutput").ap()
    vbs = nc.dram_tensor("vbs", [16, 128, 1024], BF16, kind="ExternalOutput").ap()
    wcsb = nc.dram_tensor("wcsb", [128, 128, 1024], BF16).ap()
    wqb_s = nc.dram_tensor("wqb_s", [8, 128, 1024], BF16).ap()
    wkb_s = nc.dram_tensor("wkb_s", [8, 128, 1024], BF16).ap()
    wvb_s = nc.dram_tensor("wvb_s", [8, 128, 1024], BF16).ap()

    with es:
        cx = Ctx(es, nc, f32_elems=6000, bf16_elems=80000)
        mask8 = es.enter_context(nc.sbuf_tensor("mask8", [128, 64], mybir.dt.uint8))
        xb = cx.f32.tile(8 * TBLK).rearrange("p (c t) -> p c t", c=8)
        mxt = cx.f32.tile(32)
        mnt = cx.f32.tile(32)
        cht = cx.f32.tile(32)
        sg = cx.f32.tile(32)
        xbf = cx.bf.tile(8 * TBLK).rearrange("p (c t) -> p c t", c=8)
        xn = cx.bf.tile(8 * TBLK).rearrange("p (c t) -> p c t", c=8)
        hbuf = cx.bf.tile(128 * TBLK).rearrange("p (z t) -> p z t", z=128)
        ytile = cx.bf.tile(2 * TBLK).rearrange("p (s t) -> p s t", s=2)
        dbuf = cx.bf.tile(2 * 4 * 1024).rearrange("p (s k e) -> p s k e", s=2, k=4)
        wproj = cx.bf.tile(8 * 1024).rearrange("p (c e) -> p c e", c=8)
        qkv = cx.bf.tile(3 * 2 * 1024).rearrange("p (w u e) -> p w u e", w=3, u=2)
        pm = cx.bf.tile(2 * 2048).rearrange("p (u e) -> p u e", u=2)
        msk = cx.bf.tile(2 * 16 * 64).rearrange("p (u h d) -> p u h d", u=2, h=16)
        wyb = cx.bf.tile(32 * 4 * 32).rearrange("p (y i c) -> p y i c", y=32, i=4)
        wzb = cx.bf.tile(32 * 128).rearrange("p (z c) -> p z c", z=32)

        with nc.Block() as blk:
            @blk.gpsimd
            def _(e):
                for i in range(8):
                    cx.s_ld.inc(e.dma_start(wcsb[i * 16:(i + 1) * 16],
                                            wcs[i * 16:(i + 1) * 16]), 16)
                cx.s_ld.inc(e.dma_start(wqb_s, wqT), 16)
                cx.s_ld.inc(e.dma_start(wkb_s, wkT), 16)
                cx.s_ld.inc(e.dma_start(wvb_s, wvT), 16)
                for i in range(4):
                    cx.s_ld.inc(e.dma_start(wyb[32 * i:32 * i + 32, :, i, :],
                                            wy[:, i].rearrange("y p c -> p y c")), 16)
                    cx.s_ld.inc(e.dma_start(
                        wzb[32 * i:32 * i + 32, :, :],
                        wz.rearrange("(g z) p c -> z g p c", z=4)[i]
                        .rearrange("g p c -> p g c")), 16)
            for eng in ["sync", "vector", "scalar", "tensor", "gpsimd"]:
                def _w(e):
                    cx.s_ld.wait(e)
                getattr(blk, eng)(_w)

        for tb in range(NBLK):
            emit_C_block(cx, tb, xt, ctxa, xb, xbf, xn, hbuf, ytile, dbuf, wproj,
                         qkv, pm, msk, mask8, mxt, mnt, cht, sg, wyb, wzb,
                         wcsb, wqb_s, wkb_s, wvb_s, xp, qbs, kbs, k1bs, vbs)

        with nc.Block() as blk:
            @blk.sync
            def _(e):
                cx.s_st.wait(e)
    return nc


def emit_C_block(cx, tb, xt, ctxa, xb, xbf, xn, hbuf, ytile, dbuf, wproj, qkv,
                 pm, msk, mask8, mxt, mnt, cht, sg, wyb, wzb,
                 wcsb, wqb_s, wkb_s, wvb_s, xp, qbs, kbs, k1bs, vbs):
    nc = cx.nc
    lo = tb * TBLK

    with nc.Block() as blk:
        @blk.sync
        def _(e):
            cx.s_ld.inc(e.dma_start(xb, xt.rearrange("c p t -> p c t")[:, :, lo:lo + TBLK]), 16)
            cx.s_ld.inc(e.dma_start(xbf, ctxa.rearrange("c p t -> p c t")[:, :, lo:lo + TBLK]), 16)
        @blk.vector
        def _(e):
            cx.s_ld.wait(e)
            for dc in range(8):
                e.tensor_tensor(xb[:, dc, :], xb[:, dc, :], xbf[:, dc, :], op=OP.add)
            for dc in range(8):
                op = e.tensor_copy(xbf[:, dc, :], xb[:, dc, :])
            cx.s_ev.inc(op, 1)

    emit_ln(cx, xbf, xn, TBLK)

    emit_ffn_block(cx, xb, xbf, xn, hbuf, ytile, dbuf, wyb, wzb, wcsb, xp, lo)

    emit_ln(cx, xbf, xn, TBLK)

    for w_i, (wsrc, scale) in enumerate([(wqb_s, SCALE_B), (wkb_s, None), (wvb_s, None)]):
        with nc.Block() as blk:
            @blk.sync
            def _(e, wsrc=wsrc):
                cx.s_ld.inc(e.dma_start(wproj, wsrc.rearrange("c p e -> p c e")), 16)
            @blk.tensor
            def _(e):
                cx.s_ld.wait(e)
        for tt in range(2):
            for eo in range(2):
                with nc.Block() as blk:
                    bk = [None]

                    @blk.tensor
                    def _(e, w_i=w_i, tt=tt, eo=eo):
                        bk[0] = cx.pp.acquire(e)
                        for ci in range(8):
                            mm = e.matmul(bk[0][1][:, :],
                                          xn[:, ci, tt * 128:tt * 128 + 128],
                                          wproj[:, ci, eo * 512:eo * 512 + 512],
                                          start=(ci == 0), stop=(ci == 7))
                        cx.s_mm.inc(mm, 1)

                    @blk.vector
                    def _(e, w_i=w_i, tt=tt, eo=eo, scale=scale):
                        cx.s_mm.wait(e)
                        dst = qkv[:, w_i, tt, eo * 512:eo * 512 + 512]
                        if scale is not None:
                            op = e.tensor_scalar(dst, bk[0][1][:, :], scale, None,
                                                 op0=OP.mult)
                        else:
                            op = e.tensor_copy(dst, bk[0][1][:, :])
                        cx.s_ev.inc(op, 1)
                        cx.pp.release(bk[0][0])

    for w_i in range(3):
        tview = qkv[:, w_i, :, :].rearrange("p u (h d) -> p u h d", h=16)
        mxv = mxt.rearrange("p (u h) -> p u h", u=2)
        mnv = mnt.rearrange("p (u h) -> p u h", u=2)
        with nc.Block() as blk:
            @blk.vector
            def _(e, w_i=w_i, tview=tview, mxv=mxv, mnv=mnv):
                e.tensor_reduce(mxv, tview, axis=AX.X, op=OP.max)
                cx.s_ev.inc(e.tensor_reduce(mnv, tview, axis=AX.X, op=OP.min), 1)
                cx.s_ev.wait(e)
                cx.s_ev.inc(e.tensor_tensor(sg[:, 0:32], mxt[:, 0:32], mnt[:, 0:32],
                                            op=OP.add), 1)
                cx.s_ev.wait(e)
                cx.s_ev.inc(e.tensor_scalar(mask8[:, 0:32], sg[:, 0:32], 0.0, None,
                                            op0=OP.is_gt), 1)
                cx.s_ev.wait(e)
                cx.s_ev.inc(e.select(cht[:, 0:32], mask8[:, 0:32], mxt[:, 0:32],
                                     mnt[:, 0:32]), 1)
                cx.s_ev.wait(e)
                chb = cht[:, 0:32].rearrange("p (u h) -> p u h", u=2)
                e.tensor_tensor(msk, tview, _bc4(chb), op=OP.is_equal)
                cx.s_ev.inc(e.tensor_tensor(tview, msk, tview, op=OP.mult), 1)

    emit_C_outputs(cx, tb, qkv, pm, qbs, kbs, k1bs, vbs)


def emit_ffn_block(cx, xb, xbf, xn, hbuf, ytile, dbuf, wyb, wzb, wcsb, xp, lo):
    nc = cx.nc
    for yc in range(32):
        with nc.Block() as blk:
            bky = [None]
            bkz = [None]

            @blk.tensor
            def _(e, yc=yc):
                bky[0] = cx.pp.acquire(e)
                for i in range(4):
                    mm = e.matmul(bky[0][1][32 * i:32 * i + 32, 0:TBLK],
                                  wyb[32 * i:32 * i + 32, yc, i, :],
                                  xn[32 * i:32 * i + 32, yc % 8, :],
                                  start=True, stop=True, tile_position=(32 * i, 32 * i))
                cx.s_mm.inc(mm, 1)

            @blk.vector
            def _(e, yc=yc):
                cx.s_mm.wait(e)
                cx.s_ev.inc(e.tensor_copy(ytile[:, yc % 2, :], bky[0][1][:, 0:TBLK]), 1)
                cx.pp.release(bky[0][0])

            for zi in range(4):
                zc = 32 * (yc // 8) + 4 * (yc % 8) + zi

                @blk.tensor
                def _(e, yc=yc, zi=zi, zc=zc):
                    bkz[0] = cx.pp.acquire(e)
                    cx.s_ev.wait(e)
                    cx.s_mm.inc(e.matmul(bkz[0][1][:, 0:TBLK],
                                         wzb[32 * zi:32 * zi + 32, zc // 4, :],
                                         ytile[32 * zi:32 * zi + 32, yc % 2, :],
                                         start=True, stop=True,
                                         tile_position=(32 * zi, 0)), 1)

                @blk.scalar
                def _(e, zc=zc):
                    cx.s_mm.wait(e)
                    cx.s_sc.inc(e.activation(hbuf[:, zc, :], bkz[0][1][:, 0:TBLK],
                                             AF.Gelu), 1)

                @blk.vector
                def _(e):
                    cx.s_sc.wait(e)
                    cx.s_ev.inc(e.engine_nop(), 1)
                    cx.pp.release(bkz[0][0])

    with nc.Block() as blk:
        banks = [None]

        @blk.tensor
        def _(e):
            banks[0] = cx.pp.acquire(e, nbank=8)
        for g in range(32):
            @blk.sync
            def _(e, g=g):
                if g >= 2:
                    cx.s_mm.wait(e, cx.s_mm.v - 1)
                cx.s_ld.inc(e.dma_start(dbuf[:, g % 2, :, :],
                                        wcsb[g * 4:(g + 1) * 4].rearrange("k p e -> p k e")), 16)

            @blk.tensor
            def _(e, g=g):
                cx.s_ld.wait(e)
                for ki in range(4):
                    kc = g * 4 + ki
                    for ec in range(8):
                        mm = e.matmul(banks[0][ec][1][:, 0:TBLK],
                                      dbuf[:, g % 2, ki, ec * 128:ec * 128 + 128],
                                      hbuf[:, kc, :],
                                      start=(kc == 0), stop=(kc == 127))
                cx.s_mm.inc(mm, 1)

        @blk.vector
        def _(e):
            cx.s_mm.wait(e)
            for ec in range(8):
                e.tensor_tensor(xb[:, ec, :], xb[:, ec, :], banks[0][ec][1][:, 0:TBLK],
                                op=OP.add)
            cx.s_ev.inc(e.engine_nop(), 1)
            for ec in range(8):
                cx.pp.release(banks[0][ec][0])
            for dc in range(8):
                op = e.tensor_copy(xbf[:, dc, :], xb[:, dc, :])
            cx.s_ev.inc(op, 1)

    if xp is not None:
        with nc.Block() as blk:
            @blk.sync
            def _(e):
                cx.s_st.inc(e.dma_start(xp.rearrange("c p t -> p c t")[:, :, lo:lo + TBLK],
                                        xb), 16)


def _bc4(chb):
    return chb.to_broadcast(list(chb.shape) + [64])


def emit_C_outputs(cx, tb, qkv, pm, qbs, kbs, k1bs, vbs):
    nc = cx.nc
    with nc.Block() as blk:
        @blk.sync
        def _(e):
            for u in range(2):
                cx.s_st.inc(e.dma_start(vbs[2 * tb + u], qkv[:, 2, u, :]), 16)
    specs = [(0, "pm"), (1, "pm"), (1, "unit")]
    dsts = [qbs, kbs, k1bs]
    for si, (w_i, mode) in enumerate(specs):
        with nc.Block() as blk:
            @blk.vector
            def _(e, w_i=w_i, mode=mode):
                cx.s_st.wait(e)
                src = qkv[:, w_i, :, :].rearrange("p u (h d) -> p u h d", h=16)
                dv = pm.rearrange("p u (h d) -> p u h d", h=16, d=128)
                if mode == "pm":
                    e.tensor_scalar(dv[:, :, :, 0:64], src, 0.0, None, op0=OP.max)
                    op = e.tensor_scalar(dv[:, :, :, 64:128], src, 0.0, None, op0=OP.min)
                else:
                    e.tensor_scalar(dv[:, :, :, 0:64], src, 0.0, None, op0=OP.is_gt)
                    op = e.tensor_scalar(dv[:, :, :, 64:128], src, 0.0, None, op0=OP.is_lt)
                cx.s_ev.inc(op, 1)

            @blk.sync
            def _(e, si=si):
                cx.s_ev.wait(e)
                for u in range(2):
                    cx.s_st.inc(e.dma_start(dsts[si][2 * tb + u], pm[:, u, :]), 16)


def build_launch_E():
    import contextlib
    es = contextlib.ExitStack()
    nc = bass.Bass("TRN2", target_bir_lowering=False)
    xt = nc.dram_tensor("xt", [8, 128, TOK], F32, kind="ExternalInput").ap()
    ctxe = nc.dram_tensor("ctxe", [8, 128, TOK], BF16, kind="ExternalInput").ap()
    dene = nc.dram_tensor("dene", [16, TOK], F32, kind="ExternalInput").ap()
    woT = nc.dram_tensor("woT", [8, 128, 1024], F32, kind="ExternalInput").ap()
    wy = nc.dram_tensor("wy", [32, 4, 32, 32], F32, kind="ExternalInput").ap()
    wz = nc.dram_tensor("wz", [128, 32, 128], F32, kind="ExternalInput").ap()
    wcs = nc.dram_tensor("wcs", [128, 128, 1024], F32, kind="ExternalInput").ap()
    xo = nc.dram_tensor("xo", [8, 128, TOK], F32, kind="ExternalOutput").ap()
    wcsb = nc.dram_tensor("wcsb", [128, 128, 1024], BF16).ap()

    with es:
        cx = Ctx(es, nc, f32_elems=6000, bf16_elems=80000)
        xb = cx.f32.tile(8 * TBLK).rearrange("p (c t) -> p c t", c=8)
        den = cx.f32.tile(TBLK)
        rd = cx.f32.tile(TBLK)
        xbf = cx.bf.tile(8 * TBLK).rearrange("p (c t) -> p c t", c=8)
        xn = cx.bf.tile(8 * TBLK).rearrange("p (c t) -> p c t", c=8)
        ctxn = cx.bf.tile(8 * TBLK).rearrange("p (c t) -> p c t", c=8)
        hbuf = cx.bf.tile(128 * TBLK).rearrange("p (z t) -> p z t", z=128)
        ytile = cx.bf.tile(2 * TBLK).rearrange("p (s t) -> p s t", s=2)
        dbuf = cx.bf.tile(2 * 4 * 1024).rearrange("p (s k e) -> p s k e", s=2, k=4)
        wos = cx.bf.tile(8 * 1024).rearrange("p (c e) -> p c e", c=8)
        wyb = cx.bf.tile(32 * 4 * 32).rearrange("p (y i c) -> p y i c", y=32, i=4)
        wzb = cx.bf.tile(32 * 128).rearrange("p (z c) -> p z c", z=32)
        sel16 = cx.f32.tile(1024)

        with nc.Block() as blk:
            @blk.gpsimd
            def _(e):
                cx.s_ld.inc(e.iota(sel16[:, :], [[1, 16], [0, 64]],
                                   channel_multiplier=-1,
                                   allow_small_or_imprecise_dtypes=True), 1)
            @blk.vector
            def _(e):
                cx.s_ld.wait(e)
                e.tensor_scalar(sel16[:, :], sel16[:, :], 0.0, None, op0=OP.is_equal)
            @blk.gpsimd
            def _(e):
                for i in range(8):
                    cx.s_ld.inc(e.dma_start(wcsb[i * 16:(i + 1) * 16],
                                            wcs[i * 16:(i + 1) * 16]), 16)
                cx.s_ld.inc(e.dma_start(wos, woT.rearrange("c p e -> p c e")), 16)
                for i in range(4):
                    cx.s_ld.inc(e.dma_start(wyb[32 * i:32 * i + 32, :, i, :],
                                            wy[:, i].rearrange("y p c -> p y c")), 16)
                    cx.s_ld.inc(e.dma_start(
                        wzb[32 * i:32 * i + 32, :, :],
                        wz.rearrange("(g z) p c -> z g p c", z=4)[i]
                        .rearrange("g p c -> p g c")), 16)
            for eng in ["sync", "vector", "scalar", "tensor", "gpsimd"]:
                def _w(e):
                    cx.s_ld.wait(e)
                getattr(blk, eng)(_w)

        for tb in range(NBLK):
            emit_E_block(cx, tb, xt, ctxe, dene, xb, den, rd, xbf, xn, ctxn,
                         hbuf, ytile, dbuf, wos, wyb, wzb, wcsb, xo, sel16)

        with nc.Block() as blk:
            @blk.sync
            def _(e):
                cx.s_st.wait(e)
    return nc


def emit_E_block(cx, tb, xt, ctxe, dene, xb, den, rd, xbf, xn, ctxn,
                 hbuf, ytile, dbuf, wos, wyb, wzb, wcsb, xo, sel16):
    nc = cx.nc
    lo = tb * TBLK

    with nc.Block() as blk:
        @blk.sync
        def _(e):
            cx.s_ld.inc(e.dma_start(xb, xt.rearrange("c p t -> p c t")[:, :, lo:lo + TBLK]), 16)
            cx.s_ld.inc(e.dma_start(xbf, ctxe.rearrange("c p t -> p c t")[:, :, lo:lo + TBLK]), 16)
            cx.s_ld.inc(e.dma_start(den[0:16, :], dene[:, lo:lo + TBLK]), 16)
        @blk.vector
        def _(e):
            cx.s_ld.wait(e)
            cx.s_ev.inc(e.reciprocal(rd[0:16, :], den[0:16, :]), 1)

    for dc in range(8):
        with nc.Block() as blk:
            bk = [None]

            @blk.tensor
            def _(e, dc=dc):
                bk[0] = cx.pp.acquire(e)
                for hh in range(2):
                    h = 2 * dc + hh
                    mm = e.matmul(bk[0][1][64 * hh:64 * hh + 64, 0:TBLK],
                                  sel16[0:16, 64 * h:64 * h + 64], rd[0:16, :],
                                  start=True, stop=True, tile_position=(0, 64 * hh))
                cx.s_mm.inc(mm, 1)

            @blk.vector
            def _(e, dc=dc):
                cx.s_mm.wait(e)
                cx.s_ev.inc(e.tensor_tensor(ctxn[:, dc, :], xbf[:, dc, :],
                                            bk[0][1][:, 0:TBLK], op=OP.mult), 1)
                cx.pp.release(bk[0][0])

    for ec in range(8):
        with nc.Block() as blk:
            bk = [None]

            @blk.tensor
            def _(e, ec=ec):
                bk[0] = cx.pp.acquire(e)
                for ci in range(8):
                    mm = e.matmul(bk[0][1][:, 0:TBLK], wos[:, ci, ec * 128:ec * 128 + 128],
                                  ctxn[:, ci, :], start=(ci == 0), stop=(ci == 7))
                cx.s_mm.inc(mm, 1)

            @blk.vector
            def _(e, ec=ec):
                cx.s_mm.wait(e)
                cx.s_ev.inc(e.tensor_tensor(xb[:, ec, :], xb[:, ec, :],
                                            bk[0][1][:, 0:TBLK], op=OP.add), 1)
                cx.pp.release(bk[0][0])
    with nc.Block() as blk:
        @blk.vector
        def _(e):
            for dc in range(8):
                op = e.tensor_copy(xbf[:, dc, :], xb[:, dc, :])
            cx.s_ev.inc(op, 1)

    emit_ln(cx, xbf, xn, TBLK)
    emit_ffn_block(cx, xb, xbf, xn, hbuf, ytile, dbuf, wyb, wzb, wcsb, None, lo)

    with nc.Block() as blk:
        @blk.sync
        def _(e):
            cx.s_st.inc(e.dma_start(xo.rearrange("c p t -> p c t")[:, :, lo:lo + TBLK], xb), 16)


def build_launch_D():
    import contextlib
    es = contextlib.ExitStack()
    nc = bass.Bass("TRN2", target_bir_lowering=False)
    qpm = nc.dram_tensor("qpm", [8, 128, 4096], BF16, kind="ExternalInput").ap()
    kpm = nc.dram_tensor("kpm", [8, 128, 4096], BF16, kind="ExternalInput").ap()
    ksw = nc.dram_tensor("ksw", [8, 128, 4096], BF16, kind="ExternalInput").ap()
    k1t = nc.dram_tensor("k1t", [8, 128, 32, 128], BF16, kind="ExternalInput").ap()
    vtk = nc.dram_tensor("vtk", [8, 128, 32, 64], BF16, kind="ExternalInput").ap()
    mc = nc.dram_tensor("mc", [128, 4, 512], F32, kind="ExternalInput").ap()
    lcn = nc.dram_tensor("lcn", [128, 4, 512], F32, kind="ExternalInput").ap()
    ltr = nc.dram_tensor("ltr", [128, 128], F32, kind="ExternalInput").ap()
    idn = nc.dram_tensor("idn", [128, 128], F32, kind="ExternalInput").ap()
    ctxb = nc.dram_tensor("ctxb", [128, 4, 4096], BF16, kind="ExternalOutput").ap()
    denb = nc.dram_tensor("denb", [2, 4, 4096], F32, kind="ExternalOutput").ap()

    with es:
        cx = Ctx(es, nc, f32_elems=3000, bf16_elems=62000, n_psum=6)
        bnkO = BankGuard(es, nc, "bnkO", [128, 512], F32, cx.s_ev)
        bnkP = BankGuard(es, nc, "bnkP", [128, 512], F32, cx.s_ev)
        t = {}
        t["qpm"] = cx.bf.tile(2 * 4096).rearrange("p (s x) -> p s x", s=2)
        t["kpm"] = cx.bf.tile(2 * 4096).rearrange("p (s x) -> p s x", s=2)
        t["ksw"] = cx.bf.tile(2 * 4096).rearrange("p (s x) -> p s x", s=2)
        t["k1t"] = cx.bf.tile(2 * 32 * 128).rearrange("p (s u x) -> p s u x", s=2, u=32)
        t["vraw"] = cx.bf.tile(2 * 32 * 64).rearrange("p (s u x) -> p s u x", s=2, u=32)
        t["q1"] = cx.bf.tile(4096)
        t["vext"] = cx.bf.tile(32 * 65).rearrange("p (u x) -> p u x", u=32)
        t["nvext"] = cx.bf.tile(32 * 65).rearrange("p (u x) -> p u x", u=32)
        t["ebuf"] = cx.bf.tile(4 * 512).rearrange("p (d x) -> p d x", d=4)
        t["ubuf"] = cx.bf.tile(4 * 512).rearrange("p (d x) -> p d x", d=4)
        t["zbuf"] = cx.bf.tile(512)
        t["zw"] = cx.bf.tile(512)
        t["wbuf"] = cx.bf.tile(2 * 512).rearrange("p (s x) -> p s x", s=2)
        t["ptb"] = cx.bf.tile(512)
        t["early"] = cx.bf.tile(64)
        t["state_bf"] = cx.bf.tile(65)
        t["nd"] = cx.bf.tile(4096)
        t["mc"] = cx.bf.tile(4 * 512).rearrange("p (d x) -> p d x", d=4)
        t["lcn"] = cx.bf.tile(4 * 512).rearrange("p (d x) -> p d x", d=4)
        t["ltr"] = cx.bf.tile(128)
        t["idn"] = cx.bf.tile(128)
        t["state"] = cx.f32.tile(65)
        t["corr"] = cx.f32.tile(1)
        t["pt1"] = cx.f32.tile(512)

        with nc.Block() as blk:
            @blk.gpsimd
            def _(e):
                cx.s_ld.inc(e.dma_start(t["mc"], mc.rearrange("p d x -> p d x")), 16)
                cx.s_ld.inc(e.dma_start(t["lcn"], lcn.rearrange("p d x -> p d x")), 16)
                cx.s_ld.inc(e.dma_start(t["ltr"], ltr), 16)
                cx.s_ld.inc(e.dma_start(t["idn"], idn), 16)
            @blk.sync
            def _(e):
                for nm, src in [("qpm", qpm), ("kpm", kpm), ("ksw", ksw)]:
                    cx.s_ld.inc(e.dma_start(t[nm][:, 0, :], src[0]), 16)
                cx.s_ld.inc(e.dma_start(t["k1t"][:, 0, :, :], k1t[0]), 16)
                cx.s_ld.inc(e.dma_start(t["vraw"][:, 0, :, :], vtk[0]), 16)
            for eng in ["sync", "vector", "scalar", "tensor", "gpsimd"]:
                def _w(e):
                    cx.s_ld.wait(e)
                getattr(blk, eng)(_w)

        for bh in range(8):
            emit_attn_b_bh(cx, bh, t, bnkO, bnkP, qpm, kpm, ksw, k1t, vtk,
                           ctxb, denb)

        with nc.Block() as blk:
            @blk.sync
            def _(e):
                cx.s_st.wait(e)
    return nc


def emit_attn_b_bh(cx, bh, t, bnkO, bnkP, qpm, kpm, ksw, k1t, vtk, ctxb, denb):
    nc = cx.nc
    sl = bh % 2
    hl, b = bh // 4, bh % 4

    with nc.Block() as blk:
        @blk.sync
        def _(e):
            if bh + 1 < 8:
                for nm, src in [("qpm", qpm), ("kpm", kpm), ("ksw", ksw)]:
                    cx.s_ld.inc(e.dma_start(t[nm][:, 1 - sl, :], src[bh + 1]), 16)
                cx.s_ld.inc(e.dma_start(t["k1t"][:, 1 - sl, :, :], k1t[bh + 1]), 16)
                cx.s_ld.inc(e.dma_start(t["vraw"][:, 1 - sl, :, :], vtk[bh + 1]), 16)
        @blk.vector
        def _(e):
            QPM = t["qpm"][:, sl, :]
            e.tensor_scalar(t["q1"][:, :], QPM, 0.0, None, op0=OP.not_equal)
            e.tensor_copy(t["vext"][:, :, 0:64], t["vraw"][:, sl, :, :])
            e.memset(t["vext"][:, :, 64:65], 1.0)
            e.memset(t["state"][:, :], 0.0)
            e.memset(t["state_bf"][:, :], 0.0)
            e.memset(t["corr"][:, :], 0.0)
            cx.s_ev.inc(e.engine_nop(), 1)
            cx.s_ev.wait(e)
            cx.s_ev.inc(e.tensor_scalar(t["nvext"][:, :, :], t["vext"][:, :, :],
                                        -1.0, None, op0=OP.mult), 1)

    for qc in range(8):
        emit_attn_b_qc(cx, bh, qc, t, bnkO, bnkP)

    with nc.Block() as blk:
        @blk.sync
        def _(e):
            cx.s_st.inc(e.dma_start(ctxb[64 * hl:64 * hl + 64, b, :], t["nd"][0:64, :]), 16)
        @blk.gpsimd
        def _(e):
            cx.s_st.inc(e.dma_start(denb[hl:hl + 1, b, :], t["nd"][64:65, :]), 16)


def emit_attn_b_qc(cx, bh, qc, t, bnkO, bnkP):
    nc = cx.nc
    sl = bh % 2
    lo = qc * 512
    QPM = t["qpm"][:, sl, :]
    KPM = t["kpm"][:, sl, :]
    KSW = t["ksw"][:, sl, :]
    K1T = t["k1t"][:, sl, :, :]
    marks = {}

    with nc.Block() as blk:
        banks = {}

        @blk.tensor
        def _(e):
            banks["O"] = bnkO.acquire(e)
            banks["P"] = bnkP.acquire(e)
            e.matmul(banks["O"][0:65, :], t["state_bf"][:, :], t["q1"][:, lo:lo + 512],
                     start=True, stop=False)
            for dpos in range(4):
                kt = 4 * qc + dpos
                bk = cx.pp.acquire(e)
                banks[("d", dpos)] = bk
                e.matmul(bk[1][:, :], KPM[:, kt * 128:kt * 128 + 128], QPM[:, lo:lo + 512],
                         start=True, stop=False)
                mm = e.matmul(bk[1][:, :], t["idn"][:, :], t["mc"][:, dpos, :],
                              start=False, stop=True)
                cx.s_mm.inc(mm, 1)
                marks[("d", dpos)] = cx.s_mm.v
            bz = cx.pp.acquire(e)
            banks["z"] = bz
            e.matmul(bz[1][:, :], KPM[:, 0:128], QPM[:, lo:lo + 512],
                     start=True, stop=False)
            if qc == 0:
                e.matmul(bz[1][:, :], KSW[:, 0:128], QPM[:, lo:lo + 512],
                         start=False, stop=False)
                mm = e.matmul(bz[1][:, :], t["idn"][:, :], t["mc"][:, 0, :],
                              start=False, stop=True)
            else:
                mm = e.matmul(bz[1][:, :], KSW[:, 0:128], QPM[:, lo:lo + 512],
                              start=False, stop=True)
            cx.s_mm.inc(mm, 1)
            marks["z"] = cx.s_mm.v

        @blk.scalar
        def _(e):
            for dpos in range(4):
                e.wait_ge(cx.s_mm.h, marks[("d", dpos)])
                cx.s_sc.inc(e.activation(t["ebuf"][:, dpos, :],
                                         banks[("d", dpos)][1][:, :], AF.Exp), 1)
                marks[("e", dpos)] = cx.s_sc.v
            if qc == 0:
                e.wait_ge(cx.s_mm.h, marks["z"])
                cx.s_sc.inc(e.activation(t["early"][0:64, :],
                                         banks["z"][1][0:64, 0:64], AF.Exp), 1)
            marks["exps"] = cx.s_sc.v

        @blk.vector
        def _(e):
            for dpos in range(4):
                e.wait_ge(cx.s_sc.h, marks[("e", dpos)])
                cx.s_ev.inc(e.tensor_scalar(t["ubuf"][:, dpos, :],
                                            banks[("d", dpos)][1][:, :], 0.0, None,
                                            op0=OP.is_gt), 1)
            e.wait_ge(cx.s_sc.h, marks["exps"])
            cx.s_ev.inc(e.tensor_scalar(t["zbuf"][:, :], banks["z"][1][:, :],
                                        0.0, None, op0=OP.is_equal), 1)
            marks["uz"] = cx.s_ev.v

        @blk.tensor
        def _(e):
            e.wait_ge(cx.s_ev.h, marks["uz"])
            for dpos in range(4):
                mm = e.matmul(banks["P"][0:1, :], cx.ones_col_bf[:, :],
                              t["ubuf"][:, dpos, :], start=(dpos == 0), stop=(dpos == 3))
            cx.s_mm.inc(mm, 1)
            marks["P"] = cx.s_mm.v
            bc = cx.pp.acquire(e)
            banks["c"] = bc
            e.matmul(bc[1][:, :], t["ltr"][:, :], t["zbuf"][:, :], start=True, stop=False)

        @blk.vector
        def _(e):
            e.wait_ge(cx.s_mm.h, marks["P"])
            e.tensor_copy(t["pt1"][0:1, :], banks["O"][64:65, :])
            cx.s_ev.inc(e.tensor_tensor(t["ptb"][0:1, :], t["pt1"][0:1, :],
                                        banks["P"][0:1, :], op=OP.add), 1)
            marks["pt"] = cx.s_ev.v

        @blk.tensor
        def _(e):
            e.wait_ge(cx.s_ev.h, marks["pt"])
            cx.s_mm.inc(e.matmul(banks["c"][1][:, :], cx.ones_row_bf[0:1, :],
                                 t["ptb"][0:1, :], start=False, stop=True), 1)
            marks["cz"] = cx.s_mm.v

        @blk.vector
        def _(e):
            e.wait_ge(cx.s_mm.h, marks["cz"])
            cx.s_ev.inc(e.scalar_tensor_tensor(t["zw"][:, :], banks["c"][1][:, :],
                                               64.0, t["zbuf"][:, :],
                                               op0=OP.is_le, op1=OP.mult), 1)
            e.wait_ge(cx.s_sc.h, marks["exps"])
            cx.s_ev.inc(e.engine_nop(), 1)
            cx.pp.release(banks["c"][0])
            for dpos in range(4):
                cx.pp.release(banks[("d", dpos)][0])
            cx.pp.release(banks["z"][0])
            marks["zw"] = cx.s_ev.v
            bnkP.release()

        @blk.tensor
        def _(e):
            e.wait_ge(cx.s_ev.h, marks["zw"])
            for dpos in range(4):
                kt = 4 * qc + dpos
                e.matmul(banks["O"][0:65, :], t["vext"][:, kt, :], t["ebuf"][:, dpos, :],
                         start=False, stop=False)
                e.matmul(banks["O"][0:65, :], t["vext"][:, kt, :], t["ubuf"][:, dpos, :],
                         start=False, stop=False)
                e.matmul(banks["O"][0:65, :], t["nvext"][:, kt, :], t["lcn"][:, dpos, :],
                         start=False, stop=False)
            mm = e.matmul(banks["O"][0:65, :], t["vext"][:, 0, :], t["zw"][:, :],
                          start=False, stop=(qc == 0))
            if qc == 0:
                cx.s_mm.inc(mm, 1)
            marks["att0"] = cx.s_mm.v

        for kt in range(4 * qc):
            @blk.tensor
            def _(e, kt=kt):
                bk = cx.pp.acquire(e)
                banks[("f", kt % 2)] = bk
                cx.s_mm.inc(e.matmul(bk[1][:, :], KPM[:, kt * 128:kt * 128 + 128],
                                     QPM[:, lo:lo + 512], start=True, stop=True), 1)
                marks[("fs", kt)] = cx.s_mm.v

            @blk.scalar
            def _(e, kt=kt):
                e.wait_ge(cx.s_mm.h, marks[("fs", kt)])
                if kt >= 2:
                    e.wait_ge(cx.s_mm.h, marks[("fa", kt - 2)])
                cx.s_sc.inc(e.activation(t["wbuf"][:, kt % 2, :],
                                         banks[("f", kt % 2)][1][:, :], AF.Exp), 1)
                marks[("fw", kt)] = cx.s_sc.v

            @blk.vector
            def _(e, kt=kt):
                e.wait_ge(cx.s_sc.h, marks[("fw", kt)])
                cx.s_ev.inc(e.engine_nop(), 1)
                cx.pp.release(banks[("f", kt % 2)][0])

            @blk.tensor
            def _(e, kt=kt, qc=qc):
                e.wait_ge(cx.s_sc.h, marks[("fw", kt)])
                mm = e.matmul(banks["O"][0:65, :], t["vext"][:, kt, :],
                              t["wbuf"][:, kt % 2, :],
                              start=False, stop=(kt == 4 * qc - 1))
                cx.s_mm.inc(mm, 1)
                marks[("fa", kt)] = cx.s_mm.v
        marks["attend"] = cx.s_mm.v

        @blk.vector
        def _(e):
            e.wait_ge(cx.s_mm.h, marks["attend"])
            cx.s_ev.inc(e.tensor_scalar(t["nd"][0:65, lo:lo + 512], banks["O"][0:65, :],
                                        t["corr"][0:65, 0:1], None, op0=OP.subtract), 1)
            bnkO.release()
            marks["ep"] = cx.s_ev.v

        if qc == 0:
            @blk.tensor
            def _(e):
                e.wait_ge(cx.s_ev.h, marks["ep"])
                bo2 = cx.pp.acquire(e)
                banks["o2"] = bo2
                cx.s_mm.inc(e.matmul(bo2[1][0:65, 0:64], t["vext"][0:64, 0, :],
                                     t["early"][0:64, :], start=True, stop=True), 1)
                marks["early"] = cx.s_mm.v

            @blk.vector
            def _(e):
                e.wait_ge(cx.s_mm.h, marks["early"])
                cx.s_ev.inc(e.tensor_copy(t["nd"][0:65, 0:64],
                                          banks["o2"][1][0:65, 0:64]), 1)
                cx.pp.release(banks["o2"][0])

        @blk.tensor
        def _(e):
            bs = cx.pp.acquire(e)
            banks["s"] = bs
            for i, kt in enumerate(range(4 * qc, 4 * qc + 4)):
                e.matmul(bs[1][0:128, 0:65], K1T[:, kt, :], t["vext"][:, kt, :],
                         start=(i == 0), stop=(i == 3))
            bcr = cx.pp.acquire(e)
            banks["cr"] = bcr
            e.wait_ge(cx.s_ev.h, marks["ep"])
            for i, kt in enumerate(range(4 * qc, 4 * qc + 4)):
                mm = e.matmul(bcr[1][0:65, 0:1], t["vext"][:, kt, :],
                              cx.ones_col_bf[:, :], start=(i == 0), stop=(i == 3))
            cx.s_mm.inc(mm, 1)
            marks["upd"] = cx.s_mm.v

        @blk.vector
        def _(e):
            e.wait_ge(cx.s_mm.h, marks["upd"])
            e.tensor_tensor(t["state"][:, :], t["state"][:, :], banks["s"][1][0:128, 0:65],
                            op=OP.add)
            e.tensor_tensor(t["corr"][0:65, 0:1], t["corr"][0:65, 0:1],
                            banks["cr"][1][0:65, 0:1], op=OP.add)
            cx.s_ev.inc(e.engine_nop(), 1)
            cx.s_ev.wait(e)
            cx.s_ev.inc(e.tensor_copy(t["state_bf"][:, :], t["state"][:, :]), 1)
            cx.pp.release(banks["s"][0])
            cx.pp.release(banks["cr"][0])


def host_prep_A(ins):
    x = np.asarray(ins["L0_attn_a_hidden_states"], np.float32)
    wq = np.ascontiguousarray(np.asarray(ins["L0_attn_a_Wq"], np.float32)
                              .reshape(8, 128, 1024))
    wk = np.ascontiguousarray(np.asarray(ins["L0_attn_a_Wk"], np.float32)
                              .reshape(8, 128, 1024))
    wv = np.ascontiguousarray(np.asarray(ins["L0_attn_a_Wv"], np.float32)
                              .reshape(8, 128, 1024))
    bq = np.ascontiguousarray(np.asarray(ins["L0_attn_a_bq"], np.float32)
                              .reshape(8, 128).T)
    bk = np.ascontiguousarray(np.asarray(ins["L0_attn_a_bk"], np.float32)
                              .reshape(8, 128).T)
    bv = np.asarray(ins["L0_attn_a_bv"], np.float32).reshape(1, 1024)
    maps = []
    for c in range(NCORES):
        b, h = c // 2, c % 2
        xs = x[b, h * TOK:(h + 1) * TOK, :]
        xtc = np.ascontiguousarray(xs.T).reshape(8, 128, TOK)
        maps.append(dict(xt=xtc, wq=wq, wk=wk, wv=wv, bq=bq, bk=bk, bv=bv))
    return maps


def host_prep_B(ins, outs_a):
    wq = np.asarray(ins["L0_attn_a_Wq"], np.float32)
    wk = np.asarray(ins["L0_attn_a_Wk"], np.float32)
    bq = np.asarray(ins["L0_attn_a_bq"], np.float32)
    bk = np.asarray(ins["L0_attn_a_bk"], np.float32)
    cw = np.asarray(ins["L0_attn_a_conv_w"], np.float32)[:, 0, :, 0]

    iconsts = np.zeros((128, 4, 128), np.float32)
    iconsts[:, 0, :] = np.eye(128)
    for i, a in enumerate([7.0, 15.0, 13.0]):
        blk = np.concatenate([np.eye(64), np.eye(64)], axis=1) * a
        iconsts[0:64, 1 + i, :] = blk

    maps = []
    for cc in range(NCORES):
        qam = np.stack([np.concatenate([np.asarray(outs_a[2 * b]["qa"])[cc],
                                        np.asarray(outs_a[2 * b + 1]["qa"])[cc]], axis=-1)
                        for b in range(4)], axis=1)
        kam = np.stack([np.concatenate([np.asarray(outs_a[2 * b]["ka"])[cc],
                                        np.asarray(outs_a[2 * b + 1]["ka"])[cc]], axis=-1)
                        for b in range(4)], axis=1)
        vam = np.zeros((128, 4, 32, 128), BFNP)
        for b in range(4):
            for st in range(32):
                src = np.asarray(outs_a[2 * b + st // 16]["va"])[st % 16]
                vam[:, b, st, :] = src[:, cc * 128:(cc + 1) * 128]
        xlg = np.zeros((128, 8, 4, 64), np.float32)
        for b in range(4):
            for half in range(2):
                xs = np.asarray(outs_a[2 * b + half]["xl"]).reshape(128, 8, 32)
                xlg[:, :, b, half * 32:(half + 1) * 32] = xs
        wqm = np.ascontiguousarray(
            wq[:, cc * 128:(cc + 1) * 128].reshape(8, 128, 128).transpose(1, 0, 2))
        wkm = np.ascontiguousarray(
            wk[:, cc * 128:(cc + 1) * 128].reshape(8, 128, 128).transpose(1, 0, 2))
        bqm = np.ascontiguousarray(bq[cc * 128:(cc + 1) * 128].reshape(128, 1))
        bkm = np.ascontiguousarray(bk[cc * 128:(cc + 1) * 128].reshape(128, 1))
        tb = np.zeros((128, 2, 3, 128), np.float32)
        for hl in range(2):
            wvec = cw[2 * cc + hl]
            for r in range(32):
                for so in range(128):
                    u = r - so
                    if 0 <= u < 65:
                        tb[96 + r, hl, 0, so] = wvec[u]
            for si in range(128):
                for so in range(128):
                    u = si - so + 32
                    if 0 <= u < 65:
                        tb[si, hl, 1, so] = wvec[u]
            for r in range(32):
                for so in range(128):
                    u = 160 + r - so
                    if 0 <= u < 65:
                        tb[r, hl, 2, so] = wvec[u]
        maps.append(dict(qa=qam.astype(BFNP), ka=kam.astype(BFNP), va=vam,
                         xlg=xlg, wqm=wqm, wkm=wkm, bqm=bqm, bkm=bkm,
                         tb=tb, iconsts=iconsts))
    return maps


def host_prep_C(ins, outs_b, maps_a):
    rm = np.asarray(ins["L0_ffn_rail_maker"], np.float32)
    ru = np.asarray(ins["L0_ffn_rail_up_proj"], np.float32)
    wc = np.asarray(ins["L0_ffn_w_cproj"], np.float32)
    wqb = np.asarray(ins["L0_attn_b_wq"], np.float32)
    wkb = np.asarray(ins["L0_attn_b_wk"], np.float32)
    wvb = np.asarray(ins["L0_attn_b_wv"], np.float32)

    wy = np.zeros((32, 4, 32, 32), np.float32)
    for yc in range(32):
        h = yc // 8
        for i in range(4):
            for r0 in range(2):
                rail = 8 * (yc % 8) + 2 * i + r0
                wy[yc, i, 16 * r0:16 * r0 + 16, 16 * r0:16 * r0 + 16] = rm[h, rail]
    wz = np.zeros((128, 32, 128), np.float32)
    for zc in range(128):
        h = zc // 32
        for r0 in range(2):
            rail = 2 * (zc % 32) + r0
            wz[zc, 16 * r0:16 * r0 + 16, 64 * r0:64 * r0 + 64] = ru[h, rail].T
    wcs = np.ascontiguousarray(wc.T.reshape(128, 128, 1024))
    wqT = np.ascontiguousarray(wqb.T.reshape(8, 128, 1024))
    wkT = np.ascontiguousarray(wkb.T.reshape(8, 128, 1024))
    wvT = np.ascontiguousarray(wvb.T.reshape(8, 128, 1024))

    maps = []
    for c in range(NCORES):
        b, half = c // 2, c % 2
        ctxa = np.zeros((8, 128, TOK), BFNP)
        for dc in range(8):
            ctxa[dc] = np.asarray(outs_b[dc]["ctxo"])[:, b, half * TOK:(half + 1) * TOK]
        maps.append(dict(xt=maps_a[c]["xt"], ctxa=ctxa, wy=wy, wz=wz, wcs=wcs,
                         wqT=wqT, wkT=wkT, wvT=wvT))
    return maps


def host_prep_D(outs_c):
    mc = np.zeros((128, 4, 512), np.float32)
    lcn = np.zeros((128, 4, 512), np.float32)
    for dpos in range(4):
        kj = np.arange(128)[:, None] + 128 * dpos
        qi = np.arange(512)[None, :]
        mc[:, dpos, :] = np.where(kj > qi, np.float32(-1e30), 0.0)
        lcn[:, dpos, :] = np.where(kj <= qi, np.float32(1.0), 0.0)
    ltr = (np.arange(128)[:, None] <= np.arange(128)[None, :]).astype(np.float32)
    idn = np.eye(128, dtype=np.float32)

    maps = []
    for cc in range(NCORES):
        qpm_d = np.zeros((8, 128, 4096), BFNP)
        kpm_d = np.zeros((8, 128, 4096), BFNP)
        ksw_d = np.zeros((8, 128, 4096), BFNP)
        k1t_d = np.zeros((8, 128, 32, 128), BFNP)
        vtk_d = np.zeros((8, 128, 32, 64), BFNP)
        for bh in range(8):
            hl, b = bh // 4, bh % 4
            hg = 2 * cc + hl
            for half in range(2):
                src = outs_c[2 * b + half]
                qt = np.asarray(src["qbs"])[:, :, hg * 128:(hg + 1) * 128]
                kt_ = np.asarray(src["kbs"])[:, :, hg * 128:(hg + 1) * 128]
                k1_ = np.asarray(src["k1bs"])[:, :, hg * 128:(hg + 1) * 128]
                vt_ = np.asarray(src["vbs"])[:, :, hg * 64:(hg + 1) * 64]
                tok0 = half * 2048
                qpm_d[bh, :, tok0:tok0 + 2048] = qt.reshape(2048, 128).T
                kpm_d[bh, :, tok0:tok0 + 2048] = kt_.reshape(2048, 128).T
                k1t_d[bh, :, half * 16:(half + 1) * 16, :] = \
                    k1_.transpose(1, 0, 2)
                vtk_d[bh, :, half * 16:(half + 1) * 16, :] = vt_.transpose(1, 0, 2)
            ksw_d[bh, 0:64, :] = kpm_d[bh, 64:128, :]
            ksw_d[bh, 64:128, :] = kpm_d[bh, 0:64, :]
        maps.append(dict(qpm=qpm_d, kpm=kpm_d, ksw=ksw_d, k1t=k1t_d, vtk=vtk_d,
                         mc=mc, lcn=lcn, ltr=ltr, idn=idn))
    return maps


def host_prep_E(ins, outs_c, outs_d, maps_c):
    wo = np.asarray(ins["L0_attn_b_wo"], np.float32)
    woT = np.ascontiguousarray(wo.T.reshape(8, 128, 1024))
    ctxg = np.concatenate([np.asarray(outs_d[cc]["ctxb"]) for cc in range(8)], axis=0)
    deng = np.concatenate([np.asarray(outs_d[cc]["denb"]) for cc in range(8)], axis=0)
    deng2 = np.zeros((16, 4, 4096), np.float32)
    for cc in range(8):
        deng2[2 * cc:2 * cc + 2] = np.asarray(outs_d[cc]["denb"])
    maps = []
    for c in range(NCORES):
        b, half = c // 2, c % 2
        tok = slice(half * TOK, (half + 1) * TOK)
        ctxe = np.ascontiguousarray(ctxg[:, b, tok].reshape(8, 128, TOK))
        dene = np.ascontiguousarray(deng2[:, b, tok])
        maps.append(dict(xt=np.asarray(outs_c[c]["xp"]), ctxe=ctxe, dene=dene,
                         woT=woT, wy=maps_c[c]["wy"], wz=maps_c[c]["wz"],
                         wcs=maps_c[c]["wcs"]))
    return maps


_NC_CACHE = {}


def _get_nc(name, builder):
    if name not in _NC_CACHE:
        _NC_CACHE[name] = builder()
    return _NC_CACHE[name]


def kernel(**inputs):
    ins = inputs
    cores = list(range(NCORES))
    maps_a = host_prep_A(ins)
    res_a = run_bass_kernel_spmd(_get_nc("A", build_launch_A), maps_a, core_ids=cores)
    outs_a = [res_a.results[c] for c in cores]
    maps_b = host_prep_B(ins, outs_a)
    res_b = run_bass_kernel_spmd(_get_nc("B", build_launch_B), maps_b, core_ids=cores)
    outs_b = [res_b.results[c] for c in cores]
    maps_c = host_prep_C(ins, outs_b, maps_a)
    res_c = run_bass_kernel_spmd(_get_nc("C", build_launch_C), maps_c, core_ids=cores)
    outs_c = [res_c.results[c] for c in cores]
    maps_d = host_prep_D(outs_c)
    res_d = run_bass_kernel_spmd(_get_nc("D", build_launch_D), maps_d, core_ids=cores)
    outs_d = [res_d.results[c] for c in cores]
    maps_e = host_prep_E(ins, outs_c, outs_d, maps_c)
    res_e = run_bass_kernel_spmd(_get_nc("E", build_launch_E), maps_e, core_ids=cores)
    out = np.zeros((B, S, DM), np.float32)
    for c in cores:
        b, half = c // 2, c % 2
        xo = np.asarray(res_e.results[c]["xo"], np.float32).reshape(DM, TOK)
        out[b, half * TOK:(half + 1) * TOK, :] = xo.T
    return out


# version 06 (final)
# speedup vs baseline: 1.0107x; 1.0107x over previous
import numpy as np
import ml_dtypes
import concourse.bass as bass
import concourse.mybir as mybir
from concourse.bass_utils import run_bass_kernel_spmd

F32 = mybir.dt.float32
BF16 = mybir.dt.bfloat16
AF = mybir.ActivationFunctionType
OP = mybir.AluOpType
AX = mybir.AxisListType
BFNP = ml_dtypes.bfloat16

B, S, DM = 4, 4096, 1024
TOK = 2048
NCORES = 8
HA, DA = 16, 64
SCALE_A = float(DA) ** -0.25
SCALE_B = 1.0 / 8.0
LN_EPS = 1e-5
DEBUG_B = False


class Sem:
    def __init__(self, ctx, nc, name):
        self.h = ctx.enter_context(nc.semaphore(name))
        self.v = 0

    def inc(self, instr, n=1):
        instr.then_inc(self.h, n)
        self.v += n
        return instr

    def wait(self, eng, v=None):
        eng.wait_ge(self.h, self.v if v is None else v)


class Alloc:
    def __init__(self, ctx, nc, name, n_elems, dtype):
        self.t = ctx.enter_context(nc.sbuf_tensor(name, [128, n_elems], dtype))
        self.n = n_elems
        self.off = 0

    def tile(self, w):
        assert self.off + w <= self.n, f"sbuf overflow {self.off}+{w}>{self.n}"
        ap = self.t[:, self.off:self.off + w]
        self.off += w
        return ap

    def mark(self):
        return self.off

    def reset(self, m):
        self.off = m


class PsumPool:
    def __init__(self, ctx, nc, s_ev, n=8):
        self.n = n
        self.banks = [ctx.enter_context(nc.psum_tensor(f"psb{i}", [128, 512], F32))
                      for i in range(n)]
        self.guard = [0] * n
        self.rr = 0
        self.s_ev = s_ev

    def acquire(self, te, nbank=1):
        out = []
        for _ in range(nbank):
            k = self.rr % self.n
            self.rr += 1
            if self.guard[k] > 0:
                te.wait_ge(self.s_ev.h, self.guard[k])
            out.append((k, self.banks[k]))
        return out if nbank > 1 else out[0]

    def release(self, k):
        self.guard[k] = self.s_ev.v


class BankGuard:
    def __init__(self, ctx, nc, name, shape, dtype, s_ev):
        self.t = ctx.enter_context(nc.psum_tensor(name, shape, dtype))
        self.guard = 0
        self.s_ev = s_ev

    def acquire(self, te):
        if self.guard > 0:
            te.wait_ge(self.s_ev.h, self.guard)
        return self.t

    def release(self):
        self.guard = self.s_ev.v


class Ctx:
    def __init__(self, ctx, nc, f32_elems, bf16_elems, n_psum=8):
        self.nc = nc
        self.s_ld = Sem(ctx, nc, "s_ld")
        self.s_st = Sem(ctx, nc, "s_st")
        self.s_mm = Sem(ctx, nc, "s_mm")
        self.s_ev = Sem(ctx, nc, "s_ev")
        self.s_sc = Sem(ctx, nc, "s_sc")
        self.f32 = Alloc(ctx, nc, "ar_f32", f32_elems, F32)
        self.bf = Alloc(ctx, nc, "ar_bf", bf16_elems, BF16)
        self.pp = PsumPool(ctx, nc, self.s_ev, n_psum)
        self.ones_col_bf = self.bf.tile(1)
        self.ones_row_f = self.f32.tile(128)
        self.ones_row_bf = self.bf.tile(128)
        self.ones_all_f = self.f32.tile(64)
        with nc.Block() as blk:
            @blk.vector
            def _(e):
                e.memset(self.ones_col_bf[:, :], 1.0)
                e.memset(self.ones_row_f[0:1, :], 1.0)
                e.memset(self.ones_row_bf[0:1, :], 1.0)
                e.memset(self.ones_all_f[:, :], 1.0)


def emit_ln(cx, xbf, xn, T):
    nc = cx.nc
    tw = min(T, 512)
    mf, mb = cx.f32.mark(), cx.bf.mark()
    sq8 = cx.bf.tile(8 * tw).rearrange("p (c t) -> p c t", c=8)
    stats = cx.f32.tile(2 * T)
    arow = cx.f32.tile(T)
    brow = cx.f32.tile(T)
    abc = cx.f32.tile(tw)
    bbc = cx.f32.tile(tw)
    ntc = T // tw

    for tc in range(ntc):
        lo = tc * tw
        with nc.Block() as blk:
            bk0 = bk1 = None
            for dc in range(8):
                @blk.scalar
                def _(e, dc=dc):
                    cx.s_sc.inc(e.activation(sq8[:, dc, :], xbf[:, dc, lo:lo + tw],
                                             AF.Square), 1)

                @blk.tensor
                def _(e, dc=dc):
                    nonlocal bk0, bk1
                    if dc == 0:
                        bk0 = cx.pp.acquire(e)
                        bk1 = cx.pp.acquire(e)
                    cx.s_sc.wait(e)
                    e.matmul(bk0[1][0:1, 0:tw], cx.ones_col_bf[:, :], xbf[:, dc, lo:lo + tw],
                             start=(dc == 0), stop=(dc == 7))
                    mm = e.matmul(bk1[1][0:1, 0:tw], cx.ones_col_bf[:, :], sq8[:, dc, :],
                                  start=(dc == 0), stop=(dc == 7))
                    if dc == 7:
                        cx.s_mm.inc(mm, 1)

            @blk.vector
            def _(e):
                cx.s_mm.wait(e)
                cx.s_ev.inc(e.tensor_scalar(stats[0:1, lo:lo + tw], bk0[1][0:1, 0:tw],
                                            1.0 / 1024.0, None, op0=OP.mult), 1)
                cx.pp.release(bk0[0])
                cx.s_ev.inc(e.tensor_scalar(stats[0:1, T + lo:T + lo + tw], bk1[1][0:1, 0:tw],
                                            1.0 / 1024.0, None, op0=OP.mult), 1)
                cx.pp.release(bk1[0])

    with nc.Block() as blk:
        @blk.vector
        def _(e):
            mu = stats[0:1, 0:T]
            m2 = stats[0:1, T:2 * T]
            e.tensor_tensor(brow[0:1, :], mu, mu, op=OP.mult)
            e.tensor_tensor(arow[0:1, :], m2, brow[0:1, :], op=OP.subtract)
            e.tensor_scalar(brow[0:1, :], arow[0:1, :], LN_EPS, None, op0=OP.add)
            cx.s_ev.inc(e.reciprocal(arow[0:1, :], brow[0:1, :]), 1)
    with nc.Block() as blk:
        @blk.scalar
        def _(e):
            cx.s_sc.inc(e.activation(arow[0:1, :], arow[0:1, :], AF.Sqrt), 1)
    with nc.Block() as blk:
        @blk.vector
        def _(e):
            mu = stats[0:1, 0:T]
            e.tensor_tensor(brow[0:1, :], mu, arow[0:1, :], op=OP.mult)
            e.tensor_scalar(brow[0:1, :], brow[0:1, :], -1.0, None, op0=OP.mult)

    for tc in range(ntc):
        lo = tc * tw
        with nc.Block() as blk:
            bka = bkb = None

            @blk.tensor
            def _(e):
                nonlocal bka, bkb
                bka = cx.pp.acquire(e)
                bkb = cx.pp.acquire(e)
                e.matmul(bka[1][:, 0:tw], cx.ones_row_f[0:1, :], arow[0:1, lo:lo + tw],
                         start=True, stop=True)
                cx.s_mm.inc(e.matmul(bkb[1][:, 0:tw], cx.ones_row_f[0:1, :], brow[0:1, lo:lo + tw],
                                     start=True, stop=True), 1)

            @blk.vector
            def _(e):
                cx.s_mm.wait(e)
                e.tensor_copy(abc[:, :], bka[1][:, 0:tw])
                e.tensor_copy(bbc[:, :], bkb[1][:, 0:tw])
                for dc in range(8):
                    e.tensor_tensor(xn[:, dc, lo:lo + tw], xbf[:, dc, lo:lo + tw],
                                    abc[:, :], op=OP.mult)
                    last = e.tensor_tensor(xn[:, dc, lo:lo + tw], xn[:, dc, lo:lo + tw],
                                           bbc[:, :], op=OP.add)
                cx.s_ev.inc(last, 1)
                cx.pp.release(bka[0])
                cx.pp.release(bkb[0])

    cx.f32.reset(mf)
    cx.bf.reset(mb)


def emit_proj_fm(cx, w_bf, xn, out_hbm, T, scale, bias_f32, stage):
    nc = cx.nc
    ntc = T // 512
    sidx = 0
    for do in range(8):
        for tc in range(ntc):
            lo = tc * 512
            with nc.Block() as blk:
                bk = None

                @blk.tensor
                def _(e):
                    nonlocal bk
                    bk = cx.pp.acquire(e)
                    for ci in range(8):
                        mm = e.matmul(bk[1][:, :], w_bf[:, ci, do * 128:do * 128 + 128],
                                      xn[:, ci, lo:lo + 512],
                                      start=(ci == 0), stop=(ci == 7))
                    cx.s_mm.inc(mm, 1)

                @blk.vector
                def _(e):
                    cx.s_mm.wait(e)
                    sl = sidx % 2
                    if sidx >= 2:
                        e.wait_ge(cx.s_st.h, cx.s_st.v - 16)
                    if bias_f32 is not None:
                        op = e.tensor_scalar(stage[:, sl, :], bk[1][:, :],
                                             bias_f32[:, do:do + 1], scale,
                                             op0=OP.add, op1=OP.mult)
                    else:
                        op = e.tensor_scalar(stage[:, sl, :], bk[1][:, :],
                                             scale, None, op0=OP.mult)
                    cx.s_ev.inc(op, 1)
                    cx.pp.release(bk[0])

                @blk.sync
                def _(e):
                    cx.s_ev.wait(e)
                    cx.s_st.inc(e.dma_start(out_hbm[do, :, lo:lo + 512],
                                            stage[:, sidx % 2, :]), 16)
            sidx += 1


def build_launch_A():
    import contextlib
    es = contextlib.ExitStack()
    nc = bass.Bass("TRN2", target_bir_lowering=False)
    xt = nc.dram_tensor("xt", [8, 128, TOK], F32, kind="ExternalInput").ap()
    wq = nc.dram_tensor("wq", [8, 128, 1024], F32, kind="ExternalInput").ap()
    wk = nc.dram_tensor("wk", [8, 128, 1024], F32, kind="ExternalInput").ap()
    wv = nc.dram_tensor("wv", [8, 128, 1024], F32, kind="ExternalInput").ap()
    bq = nc.dram_tensor("bq", [128, 8], F32, kind="ExternalInput").ap()
    bk_ = nc.dram_tensor("bk", [128, 8], F32, kind="ExternalInput").ap()
    bv = nc.dram_tensor("bv", [1, 1024], F32, kind="ExternalInput").ap()
    qa = nc.dram_tensor("qa", [8, 128, TOK], BF16, kind="ExternalOutput").ap()
    ka = nc.dram_tensor("ka", [8, 128, TOK], BF16, kind="ExternalOutput").ap()
    va = nc.dram_tensor("va", [16, 128, 1024], BF16, kind="ExternalOutput").ap()
    xl = nc.dram_tensor("xl", [128, 256], F32, kind="ExternalOutput").ap()

    with es:
        cx = Ctx(es, nc, f32_elems=6 * TOK + 2048, bf16_elems=2 * 8 * TOK + 8 * 1024 * 3 + 8192)
        xbf_t = cx.bf.tile(8 * TOK)
        xbf = xbf_t.rearrange("p (c t) -> p c t", c=8)
        xn_t = cx.bf.tile(8 * TOK)
        xn = xn_t.rearrange("p (c t) -> p c t", c=8)
        wqb = cx.bf.tile(8 * 1024).rearrange("p (c d) -> p c d", c=8)
        wkb = cx.bf.tile(8 * 1024).rearrange("p (c d) -> p c d", c=8)
        wvb = cx.bf.tile(8 * 1024).rearrange("p (c d) -> p c d", c=8)
        stage = cx.bf.tile(3 * 512).rearrange("p (s t) -> p s t", s=3)[:, 0:2, :]
        stage2 = cx.bf.tile(2 * 512).rearrange("p (s t) -> p s t", s=2)
        bvrow = cx.bf.tile(1024)
        bqs = cx.f32.tile(8)
        bks = cx.f32.tile(8)
        xls = cx.f32.tile(256)
        xls3 = xls.rearrange("p (c l) -> p c l", c=8)

        with nc.Block() as blk:
            @blk.gpsimd
            def _(e):
                cx.s_ld.inc(e.dma_start(xbf, xt.rearrange("c p t -> p c t")), 16)
                cx.s_ld.inc(e.dma_start(wqb, wq.rearrange("c p d -> p c d")), 16)
                cx.s_ld.inc(e.dma_start(wkb, wk.rearrange("c p d -> p c d")), 16)
                cx.s_ld.inc(e.dma_start(wvb, wv.rearrange("c p d -> p c d")), 16)
                cx.s_ld.inc(e.dma_start(bvrow[0:1, :], bv[:, :]), 16)
            @blk.sync
            def _(e):
                cx.s_ld.inc(e.dma_start(bqs[:, :], bq[:, :]), 16)
                cx.s_ld.inc(e.dma_start(bks[:, :], bk_[:, :]), 16)
                cx.s_ld.wait(e)
            @blk.vector
            def _(e):
                cx.s_ld.wait(e)
            @blk.scalar
            def _(e):
                cx.s_ld.wait(e)
            @blk.tensor
            def _(e):
                cx.s_ld.wait(e)

        emit_ln(cx, xbf, xn, TOK)

        with nc.Block() as blk:
            @blk.vector
            def _(e):
                for dc in range(8):
                    e.reduce_sum(xls3[:, dc, :],
                                 xn[:, dc, :].rearrange("p (l w) -> p l w", w=64),
                                 axis=AX.X)
                e.tensor_scalar(xls[:, :], xls[:, :], 1.0 / 64.0, None, op0=OP.mult)
            @blk.sync
            def _(e):
                pass
        with nc.Block() as blk:
            @blk.sync
            def _(e):
                cx.s_st.inc(e.dma_start(xl[:, :], xls[:, :]), 16)

        emit_proj_fm(cx, wqb, xn, qa, TOK, SCALE_A, bqs, stage)
        emit_proj_fm(cx, wkb, xn, ka, TOK, SCALE_A, bks, stage)

        sidx = 0
        for tt in range(16):
            for eo in range(2):
                lo = eo * 512
                with nc.Block() as blk:
                    bk2 = None

                    @blk.tensor
                    def _(e):
                        nonlocal bk2
                        bk2 = cx.pp.acquire(e)
                        for ci in range(8):
                            e.matmul(bk2[1][:, :], xn[:, ci, tt * 128:tt * 128 + 128],
                                     wvb[:, ci, lo:lo + 512],
                                     start=(ci == 0), stop=False)
                        cx.s_mm.inc(e.matmul(bk2[1][:, :], cx.ones_row_bf[0:1, 0:128],
                                             bvrow[0:1, lo:lo + 512],
                                             start=False, stop=True), 1)

                    @blk.vector
                    def _(e):
                        cx.s_mm.wait(e)
                        if sidx >= 2:
                            e.wait_ge(cx.s_st.h, cx.s_st.v - 16)
                        cx.s_ev.inc(e.tensor_copy(stage2[:, sidx % 2, :], bk2[1][:, :]), 1)
                        cx.pp.release(bk2[0])

                    @blk.sync
                    def _(e):
                        cx.s_ev.wait(e)
                        cx.s_st.inc(e.dma_start(va[tt, :, lo:lo + 512],
                                                stage2[:, sidx % 2, :]), 16)
                sidx += 1

        with nc.Block() as blk:
            @blk.sync
            def _(e):
                cx.s_st.wait(e)
    return nc


def build_launch_B():
    import contextlib
    es = contextlib.ExitStack()
    nc = bass.Bass("TRN2", target_bir_lowering=False)
    qa = nc.dram_tensor("qa", [128, 4, 4096], BF16, kind="ExternalInput").ap()
    ka = nc.dram_tensor("ka", [128, 4, 4096], BF16, kind="ExternalInput").ap()
    va = nc.dram_tensor("va", [128, 4, 32, 128], BF16, kind="ExternalInput").ap()
    xlg = nc.dram_tensor("xlg", [128, 8, 4, 64], F32, kind="ExternalInput").ap()
    wqm = nc.dram_tensor("wqm", [128, 8, 128], F32, kind="ExternalInput").ap()
    wkm = nc.dram_tensor("wkm", [128, 8, 128], F32, kind="ExternalInput").ap()
    bqm = nc.dram_tensor("bqm", [128, 1], F32, kind="ExternalInput").ap()
    bkm = nc.dram_tensor("bkm", [128, 1], F32, kind="ExternalInput").ap()
    tb = nc.dram_tensor("tb", [128, 2, 3, 128], F32, kind="ExternalInput").ap()
    ic = nc.dram_tensor("iconsts", [128, 4, 128], F32, kind="ExternalInput").ap()
    ctxo = nc.dram_tensor("ctxo", [128, 4, 4096], BF16, kind="ExternalOutput").ap()
    dbg = None
    if DEBUG_B:
        dbg = nc.dram_tensor("dbg", [128, 16, 512], F32, kind="ExternalOutput").ap()

    with es:
        cx = Ctx(es, nc, f32_elems=3072,
                 bf16_elems=3 * 4 * 4096 + 4096 + 2 * 2048 + 8 * 512 + 16 * 128 + 4096,
                 n_psum=6)
        tp = BankGuard(es, nc, "tp_bf", [128, 1024], BF16, cx.s_ev)
        qam = cx.bf.tile(4 * 4096).rearrange("p (b s) -> p b s", b=4)
        kam = cx.bf.tile(4 * 4096).rearrange("p (b s) -> p b s", b=4)
        vam = cx.bf.tile(4 * 4096).rearrange("p (b t d) -> p b t d", b=4, t=32)
        E1 = cx.bf.tile(4096)
        E3h = [cx.bf.tile(2048).rearrange("p (t l) -> p t l", t=32) for _ in range(2)]
        xls = cx.bf.tile(8 * 4 * 64).rearrange("p (c b l) -> p c b l", c=8, b=4)
        wqs = cx.bf.tile(8 * 128).rearrange("p (c d) -> p c d", c=8)
        wks = cx.bf.tile(8 * 128).rearrange("p (c d) -> p c d", c=8)
        tbs = cx.bf.tile(2 * 3 * 128).rearrange("p (h r f) -> p h r f", h=2, r=3)
        ident = cx.bf.tile(128)
        ipair = cx.bf.tile(3 * 128).rearrange("p (i l) -> p i l", i=3)
        stage = cx.bf.tile(2 * 512).rearrange("p (s t) -> p s t", s=2)
        w = {}
        for nm in ["qlp", "klp", "e2t", "K2p"]:
            w[nm] = cx.bf.tile(64)
        for nm in ["K2Tw", "Vw", "VwT", "Gw", "GwT", "t1w", "tTw", "t3w", "t5w",
                   "Vnw", "k3vw", "den3sb", "M2pad"]:
            w[nm] = cx.bf.tile(128)
        ones_f = cx.f32.tile(64)
        w["rmx"] = cx.f32.tile(2)
        w["rmxr"] = cx.f32.tile(2)
        w["r3"] = cx.f32.tile(2)
        w["r3r"] = cx.f32.tile(2)
        w["denrow2"] = cx.f32.tile(512)
        w["d2"] = cx.f32.tile(1)
        w["r2"] = cx.f32.tile(1)
        w["denrow"] = cx.f32.tile(512)
        w["ones_f"] = ones_f
        w["ident"] = ident
        w["ipair"] = ipair
        w["tp"] = tp
        w["E3h_l"] = E3h
        w["E1"] = E1

        with nc.Block() as blk:
            @blk.gpsimd
            def _(e):
                cx.s_ld.inc(e.dma_start(xls, xlg.rearrange("p c b l -> p c b l")), 16)
                cx.s_ld.inc(e.dma_start(wqs, wqm.rearrange("p c d -> p c d")), 16)
                cx.s_ld.inc(e.dma_start(wks, wkm.rearrange("p c d -> p c d")), 16)
                cx.s_ld.inc(e.dma_start(tbs, tb.rearrange("p h r f -> p h r f")), 16)
                cx.s_ld.inc(e.dma_start(ident[:, :], ic[:, 0, :]), 16)
                cx.s_ld.inc(e.dma_start(ipair.rearrange("p i l -> p i l"),
                                        ic[:, 1:4, :]), 16)
                cx.s_ld.inc(e.dma_start(w["rmx"][:, 0:1], bqm[:, :]), 16)
                cx.s_ld.inc(e.dma_start(w["rmx"][:, 1:2], bkm[:, :]), 16)
            @blk.sync
            def _(e):
                cx.s_ld.inc(e.dma_start(qam, qa.rearrange("p b s -> p b s")), 16)
                cx.s_ld.inc(e.dma_start(kam, ka.rearrange("p b s -> p b s")), 16)
                cx.s_ld.inc(e.dma_start(vam, va.rearrange("p b t d -> p b t d")), 16)
                cx.s_ld.wait(e)
            @blk.vector
            def _(e):
                cx.s_ld.wait(e)
                e.memset(ones_f[:, :], 1.0)
                e.memset(w["M2pad"][:, :], 0.0)
            @blk.tensor
            def _(e):
                cx.s_ld.wait(e)
            @blk.scalar
            def _(e):
                cx.s_ld.wait(e)
            @blk.gpsimd
            def _(e):
                cx.s_ld.wait(e)
        bq_t = cx.f32.tile(1)
        bk_t = cx.f32.tile(1)
        with nc.Block() as blk:
            @blk.vector
            def _(e):
                e.tensor_copy(bq_t[:, :], w["rmx"][:, 0:1])
                e.tensor_copy(bk_t[:, :], w["rmx"][:, 1:2])
        w["bq"] = bq_t
        w["bk"] = bk_t

        w["dbg"] = dbg
        for b in range(4):
            emit_attn_a_batch(cx, b, w, qam, kam, vam, xls, wqs, wks, tbs, stage, ctxo)

        with nc.Block() as blk:
            @blk.sync
            def _(e):
                cx.s_st.wait(e)
    return nc


def emit_wide_product(cx, lhsTw, rhsw, out_sb=None, stt_const=None, stt_out=None,
                      scale=None):
    nc = cx.nc
    with nc.Block() as blk:
        bk = [None]

        @blk.tensor
        def _(e):
            bk[0] = cx.pp.acquire(e)
            for h in range(2):
                mm = e.matmul(bk[0][1][0:64, 64 * h:64 * h + 64],
                              lhsTw[0:64, 64 * h:64 * h + 64],
                              rhsw[0:64, 64 * h:64 * h + 64],
                              start=True, stop=True)
            cx.s_mm.inc(mm, 1)

        @blk.vector
        def _(e):
            cx.s_mm.wait(e)
            if stt_out is not None:
                op = e.scalar_tensor_tensor(stt_out[0:64, :], bk[0][1][0:64, 0:128],
                                            -1.0, stt_const[0:64, :],
                                            op0=OP.mult, op1=OP.add)
            elif scale is not None:
                op = e.tensor_scalar(out_sb[0:64, :], bk[0][1][0:64, 0:128], scale,
                                     None, op0=OP.mult)
            else:
                op = e.tensor_copy(out_sb[0:64, :], bk[0][1][0:64, 0:128])
            cx.s_ev.inc(op, 1)
            cx.pp.release(bk[0][0])


def emit_transpose_wide(cx, w, src, dst):
    nc = cx.nc
    tp = w["tp"]
    with nc.Block() as blk:
        @blk.tensor
        def _(e):
            pb = tp.acquire(e)
            e.transpose(pb[0:64, 0:64], src[0:64, 0:64], w["ident"][0:64, 0:64])
            cx.s_mm.inc(e.transpose(pb[0:64, 64:128], src[0:64, 64:128],
                                    w["ident"][0:64, 0:64]), 1)

        @blk.vector
        def _(e):
            cx.s_mm.wait(e)
            cx.s_ev.inc(e.tensor_copy(dst[0:64, 0:128], tp.t[0:64, 0:128]), 1)
            tp.release()


def emit_dbg(cx, w, slot, src, pad2=False):
    if w.get("dbg") is None:
        return
    nc = cx.nc
    with nc.Block() as blk:
        @blk.gpsimd
        def _(e):
            n = src.shape[-1] if len(src.shape) == 2 else 512
            with nc.allow_non_contiguous_dma(reason="debug"):
                cx.s_st.inc(e.dma_start(w["dbg"][0:src.shape[0], slot, 0:n], src), 16)


def emit_attn_a_batch(cx, b, w, qam, kam, vam, xls, wqs, wks, tbs, stage, ctxo):
    nc = cx.nc
    with nc.Block() as blk:
        bkq = [None]
        bkk = [None]

        @blk.tensor
        def _(e):
            bkq[0] = cx.pp.acquire(e)
            bkk[0] = cx.pp.acquire(e)
            for ci in range(8):
                e.matmul(bkq[0][1][:, 0:64], wqs[:, ci, :], xls[:, ci, b, :],
                         start=(ci == 0), stop=(ci == 7))
                mm = e.matmul(bkk[0][1][:, 0:64], wks[:, ci, :], xls[:, ci, b, :],
                              start=(ci == 0), stop=(ci == 7))
            cx.s_mm.inc(mm, 1)

        @blk.vector
        def _(e):
            cx.s_mm.wait(e)
            cx.s_ev.inc(e.tensor_scalar(w["qlp"][:, :], bkq[0][1][:, 0:64],
                                        w["bq"][:, 0:1], SCALE_A,
                                        op0=OP.add, op1=OP.mult), 1)
            cx.pp.release(bkq[0][0])
            cx.s_ev.inc(e.tensor_scalar(w["klp"][:, :], bkk[0][1][:, 0:64],
                                        w["bk"][:, 0:1], SCALE_A,
                                        op0=OP.add, op1=OP.mult), 1)
            cx.pp.release(bkk[0][0])

    if b == 0:
        emit_dbg(cx, w, 0, w["qlp"][:, 0:64])
        emit_dbg(cx, w, 1, w["klp"][:, 0:64])
    with nc.Block() as blk:
        bk2 = [None]

        @blk.tensor
        def _(e):
            bk2[0] = cx.pp.acquire(e)
            e.matmul(bk2[0][1][0:64, 0:64], w["qlp"][0:64, :], w["klp"][0:64, :],
                     start=True, stop=True, tile_position=(0, 0))
            cx.s_mm.inc(e.matmul(bk2[0][1][64:128, 0:64], w["qlp"][64:128, :],
                                 w["klp"][64:128, :], start=True, stop=True,
                                 tile_position=(64, 64)), 1)

        @blk.scalar
        def _(e):
            cx.s_mm.wait(e)
            cx.s_sc.inc(e.activation(w["e2t"][:, :], bk2[0][1][:, 0:64], AF.Exp,
                                     accum_out=w["d2"][:, 0:1]), 1)

        @blk.vector
        def _(e):
            cx.s_sc.wait(e)
            cx.s_ev.inc(e.reciprocal(w["r2"][:, 0:1], w["d2"][:, 0:1]), 1)
            cx.s_ev.wait(e)
            cx.s_ev.inc(e.tensor_scalar(w["K2p"][:, :], w["e2t"][:, :],
                                        w["r2"][:, 0:1], None, op0=OP.mult), 1)
            cx.pp.release(bk2[0][0])

    with nc.Block() as blk:
        @blk.tensor
        def _(e):
            pb = w["tp"].acquire(e)
            cx.s_mm.inc(e.transpose(pb[0:64, 0:128], w["K2p"][:, :], w["ident"][:, :]), 1)

        @blk.vector
        def _(e):
            cx.s_mm.wait(e)
            cx.s_ev.inc(e.tensor_copy(w["K2Tw"][0:64, :], w["tp"].t[0:64, 0:128]), 1)
            w["tp"].release()

    if b == 0:
        emit_dbg(cx, w, 2, w["K2p"][:, 0:64])
        emit_dbg(cx, w, 3, w["K2Tw"][0:64, 0:128])
        emit_dbg(cx, w, 14, w["d2"][:, 0:1], pad2=True)
        emit_dbg(cx, w, 15, w["e2t"][:, 0:64])
    with nc.Block() as blk:
        bkc = [None]

        @blk.tensor
        def _(e):
            bkc[0] = cx.pp.acquire(e)
            e.matmul(bkc[0][1][0:1, 0:64], cx.ones_col_bf[0:64, 0:1], w["K2p"][0:64, :],
                     start=True, stop=True, tile_position=(0, 0))
            cx.s_mm.inc(e.matmul(bkc[0][1][32:33, 0:64], cx.ones_col_bf[64:128, 0:1],
                                 w["K2p"][64:128, :], start=True, stop=True,
                                 tile_position=(64, 32)), 1)

        @blk.vector
        def _(e):
            cx.s_mm.wait(e)
            e.tensor_reduce(w["rmx"][0:1, 0:1], bkc[0][1][0:1, 0:64], axis=AX.X, op=OP.max)
            cx.s_ev.inc(e.tensor_reduce(w["rmx"][32:33, 0:1], bkc[0][1][32:33, 0:64],
                                        axis=AX.X, op=OP.max), 1)
            cx.s_ev.wait(e)
            e.reciprocal(w["rmxr"][0:1, 0:1], w["rmx"][0:1, 0:1])
            cx.s_ev.inc(e.reciprocal(w["rmxr"][32:33, 0:1], w["rmx"][32:33, 0:1]), 1)
            cx.pp.release(bkc[0][0])

    with nc.Block() as blk:
        bka = [None]
        bkb = [None]

        @blk.tensor
        def _(e):
            bka[0] = cx.pp.acquire(e)
            bkb[0] = cx.pp.acquire(e)
            e.matmul(bka[0][1][0:64, 0:1], w["ones_f"][0:1, :], w["rmxr"][0:1, 0:1],
                     start=True, stop=True)
            cx.s_mm.inc(e.matmul(bkb[0][1][0:64, 0:1], w["ones_f"][32:33, :],
                                 w["rmxr"][32:33, 0:1], start=True, stop=True,
                                 tile_position=(32, 0)), 1)

        @blk.vector
        def _(e):
            cx.s_mm.wait(e)
            e.tensor_copy(w["rmx"][0:64, 0:1], bka[0][1][0:64, 0:1])
            cx.s_ev.inc(e.tensor_copy(w["rmx"][0:64, 1:2], bkb[0][1][0:64, 0:1]), 1)
            cx.pp.release(bka[0][0])
            cx.pp.release(bkb[0][0])

    with nc.Block() as blk:
        @blk.vector
        def _(e):
            for h in range(2):
                op = e.tensor_scalar(w["Vw"][0:64, 64 * h:64 * h + 64],
                                     w["K2Tw"][0:64, 64 * h:64 * h + 64],
                                     w["rmx"][0:64, h:h + 1], None, op0=OP.mult)
            cx.s_ev.inc(op, 1)
    emit_transpose_wide(cx, w, w["Vw"], w["VwT"])

    for it in range(6):
        emit_wide_product(cx, w["K2Tw"], w["Vw"], out_sb=w["Gw"])
        emit_transpose_wide(cx, w, w["Gw"], w["GwT"])
        with nc.Block() as blk:
            @blk.vector
            def _(e):
                cx.s_ev.inc(e.scalar_tensor_tensor(w["t1w"][0:64, :], w["Gw"][0:64, :],
                                                   -1.0, w["ipair"][0:64, 0, :],
                                                   op0=OP.mult, op1=OP.add), 1)
        emit_wide_product(cx, w["GwT"], w["t1w"], stt_const=w["ipair"][:, 1, :],
                          stt_out=w["t3w"])
        emit_wide_product(cx, w["GwT"], w["t3w"], stt_const=w["ipair"][:, 2, :],
                          stt_out=w["t5w"])
        emit_wide_product(cx, w["VwT"], w["t5w"], out_sb=w["Vnw"], scale=0.25)
        with nc.Block() as blk:
            @blk.vector
            def _(e):
                cx.s_ev.inc(e.tensor_copy(w["Vw"][0:64, :], w["Vnw"][0:64, :]), 1)
        emit_transpose_wide(cx, w, w["Vw"], w["VwT"])

    if b == 0:
        emit_dbg(cx, w, 4, w["Vw"][0:64, 0:128])
        emit_dbg(cx, w, 5, w["VwT"][0:64, 0:128])
        emit_dbg(cx, w, 6, w["rmx"][0:64, 0:2])
    for sg in range(8):
        with nc.Block() as blk:
            bh = [None, None]

            @blk.tensor
            def _(e):
                bh[0] = cx.pp.acquire(e)
                bh[1] = cx.pp.acquire(e)
                for sti in range(4):
                    st = sg * 4 + sti
                    for h in range(2):
                        mm = e.matmul(bh[h][1][:, 64 * sti:64 * sti + 64],
                                      kam[64 * h:64 * h + 64, b, st * 128:st * 128 + 128],
                                      w["qlp"][64 * h:64 * h + 64, :],
                                      start=True, stop=True,
                                      tile_position=(64 * h, 0))
                cx.s_mm.inc(mm, 1)

            @blk.scalar
            def _(e):
                cx.s_mm.wait(e)
                for h in range(2):
                    src = bh[h][1][:, 0:256].rearrange("p (t l) -> p t l", t=4)
                    cx.s_sc.inc(e.activation(E3_dst(w, h, sg), src, AF.Exp), 1)

            @blk.vector
            def _(e):
                cx.s_sc.wait(e)
                cx.s_ev.inc(e.engine_nop(), 1)
                cx.pp.release(bh[0][0])
                cx.pp.release(bh[1][0])

    with nc.Block() as blk:
        bkd = [None]
        bkv = [None]

        @blk.tensor
        def _(e):
            bkd[0] = cx.pp.acquire(e)
            bkv[0] = cx.pp.acquire(e)
            for st in range(32):
                for h in range(2):
                    mm = e.matmul(bkd[0][1][0:1, 64 * h:64 * h + 64],
                                  cx.ones_col_bf[:, 0:1], w["E3h_l"][h][:, st, :],
                                  start=(st == 0), stop=(st == 31))
            cx.s_mm.inc(mm, 1)
            for st in range(32):
                for h in range(2):
                    mm = e.matmul(bkv[0][1][0:64, 64 * h:64 * h + 64],
                                  w["E3h_l"][h][:, st, :],
                                  vam[:, b, st, 64 * h:64 * h + 64],
                                  start=(st == 0), stop=(st == 31))
            cx.s_mm.inc(mm, 1)

        @blk.vector
        def _(e):
            cx.s_mm.wait(e)
            cx.s_ev.inc(e.tensor_copy(w["den3sb"][0:1, 0:128], bkd[0][1][0:1, 0:128]), 1)
            cx.pp.release(bkd[0][0])

        @blk.tensor
        def _(e):
            cx.s_ev.wait(e)
            bkd[0] = cx.pp.acquire(e)
            for h in range(2):
                mm = e.matmul(bkd[0][1][0:64, h:h + 1],
                              w["den3sb"][0:1, 64 * h:64 * h + 64],
                              cx.ones_row_bf[0:1, 0:1], start=True, stop=True)
            cx.s_mm.inc(mm, 1)

        @blk.vector
        def _(e):
            cx.s_mm.wait(e)
            cx.s_ev.inc(e.tensor_copy(w["r3r"][0:64, 0:2], bkd[0][1][0:64, 0:2]), 1)
            cx.s_ev.wait(e)
            e.reciprocal(w["r3"][0:64, 0:1], w["r3r"][0:64, 0:1])
            cx.s_ev.inc(e.reciprocal(w["r3"][0:64, 1:2], w["r3r"][0:64, 1:2]), 1)
            cx.s_ev.wait(e)
            for h in range(2):
                op = e.tensor_scalar(w["k3vw"][0:64, 64 * h:64 * h + 64],
                                     bkv[0][1][0:64, 64 * h:64 * h + 64],
                                     w["r3"][0:64, h:h + 1], None, op0=OP.mult)
            cx.s_ev.inc(op, 1)
            cx.pp.release(bkd[0][0])
            cx.pp.release(bkv[0][0])

    if b == 0:
        emit_dbg(cx, w, 7, w["k3vw"][0:64, 0:128])
        emit_dbg(cx, w, 8, w["den3sb"][0:1, 0:128])
        emit_dbg(cx, w, 9, w["r3r"][0:64, 0:2], pad2=True)
        emit_dbg(cx, w, 15, w["r3"][0:64, 0:2], pad2=True)
    with nc.Block() as blk:
        bkm = [None]

        @blk.tensor
        def _(e):
            bkm[0] = cx.pp.acquire(e)
            for h in range(2):
                mm = e.matmul(bkm[0][1][64 * h:64 * h + 64, 0:64],
                              w["VwT"][0:64, 64 * h:64 * h + 64],
                              w["k3vw"][0:64, 64 * h:64 * h + 64],
                              start=True, stop=True, tile_position=(0, 64 * h))
            cx.s_mm.inc(mm, 1)

        @blk.vector
        def _(e):
            cx.s_mm.wait(e)
            for h in range(2):
                op = e.tensor_copy(w["M2pad"][64 * h:64 * h + 64, 64 * h:64 * h + 64],
                                   bkm[0][1][64 * h:64 * h + 64, 0:64])
            cx.s_ev.inc(op, 1)
            cx.pp.release(bkm[0][0])

    if b == 0:
        emit_dbg(cx, w, 10, w["M2pad"][:, 0:128])
    for sc in range(8):
        with nc.Block() as blk:
            bke = [None]
            bkn = [None]
            bkr = [None]

            @blk.tensor
            def _(e):
                bke[0] = cx.pp.acquire(e)
                e.matmul(bke[0][1][0:64, :], w["klp"][0:64, :],
                         qam[0:64, b, sc * 512:sc * 512 + 512],
                         start=True, stop=True, tile_position=(0, 0))
                cx.s_mm.inc(e.matmul(bke[0][1][64:128, :], w["klp"][64:128, :],
                                     qam[64:128, b, sc * 512:sc * 512 + 512],
                                     start=True, stop=True, tile_position=(64, 64)), 1)

            @blk.scalar
            def _(e):
                cx.s_mm.wait(e)
                cx.s_sc.inc(e.activation(w["E1"][:, sc * 512:sc * 512 + 512],
                                         bke[0][1][:, :], AF.Exp), 1)

            @blk.tensor
            def _(e):
                cx.s_sc.wait(e)
                bkn[0] = cx.pp.acquire(e)
                e.matmul(bkn[0][1][0:1, :], cx.ones_col_bf[0:64, 0:1],
                         w["E1"][0:64, sc * 512:sc * 512 + 512],
                         start=True, stop=True, tile_position=(0, 0))
                cx.s_mm.inc(e.matmul(bkn[0][1][32:33, :], cx.ones_col_bf[64:128, 0:1],
                                     w["E1"][64:128, sc * 512:sc * 512 + 512],
                                     start=True, stop=True, tile_position=(64, 32)), 1)

            @blk.vector
            def _(e):
                cx.s_mm.wait(e)
                e.tensor_copy(w["denrow2"][0:1, :], bkn[0][1][0:1, :])
                cx.s_ev.inc(e.tensor_copy(w["denrow2"][32:33, :], bkn[0][1][32:33, :]), 1)
                cx.s_ev.wait(e)
                e.reciprocal(w["denrow"][0:1, :], w["denrow2"][0:1, :])
                cx.s_ev.inc(e.reciprocal(w["denrow"][32:33, :], w["denrow2"][32:33, :]), 1)
                cx.pp.release(bke[0][0])
                cx.pp.release(bkn[0][0])

            @blk.tensor
            def _(e):
                cx.s_ev.wait(e)
                bkr[0] = cx.pp.acquire(e)
                e.matmul(bkr[0][1][0:64, :], w["ones_f"][0:1, :], w["denrow"][0:1, :],
                         start=True, stop=True, tile_position=(0, 0))
                cx.s_mm.inc(e.matmul(bkr[0][1][64:128, :], w["ones_f"][32:33, :],
                                     w["denrow"][32:33, :], start=True, stop=True,
                                     tile_position=(32, 64)), 1)

            @blk.vector
            def _(e):
                cx.s_mm.wait(e)
                cx.s_ev.inc(e.tensor_tensor(w["E1"][:, sc * 512:sc * 512 + 512],
                                            w["E1"][:, sc * 512:sc * 512 + 512],
                                            bkr[0][1][:, :], op=OP.mult), 1)
                cx.pp.release(bkr[0][0])

    if b == 0:
        emit_dbg(cx, w, 11, w["E1"][:, 0:512])
        emit_dbg(cx, w, 12, w["denrow"][0:1, 0:512])
        emit_dbg(cx, w, 13, w["denrow"][32:33, 0:512])
    for sc in range(8):
        with nc.Block() as blk:
            bkc2 = [None]

            @blk.tensor
            def _(e):
                bkc2[0] = cx.pp.acquire(e)
                ps = bkc2[0][1]
                e.matmul(ps[0:64, :], w["M2pad"][:, 0:64],
                         w["E1"][:, sc * 512:sc * 512 + 512],
                         start=True, stop=False, tile_position=(0, 0))
                e.matmul(ps[64:128, :], w["M2pad"][:, 64:128],
                         w["E1"][:, sc * 512:sc * 512 + 512],
                         start=True, stop=False, tile_position=(0, 64))
                mms = []
                for sti in range(4):
                    st = sc * 4 + sti
                    for h in range(2):
                        co = 64 * h
                        sub = ps[co:co + 64, 128 * sti:128 * sti + 128]
                        if st > 0:
                            mms.append((sub, vam[:, b, st - 1, co:co + 64],
                                        tbs[:, h, 0, :], co))
                        mms.append((sub, vam[:, b, st, co:co + 64],
                                    tbs[:, h, 1, :], co))
                        if st < 31:
                            mms.append((sub, vam[:, b, st + 1, co:co + 64],
                                        tbs[:, h, 2, :], co))
                for i, (sub, lh, rh, co) in enumerate(mms):
                    mm = e.matmul(sub, lh, rh, start=False, stop=(i == len(mms) - 1),
                                  tile_position=(0, co))
                cx.s_mm.inc(mm, 1)

            @blk.vector
            def _(e):
                cx.s_mm.wait(e)
                if b * 8 + sc >= 2:
                    e.wait_ge(cx.s_st.h, cx.s_st.v - 16)
                cx.s_ev.inc(e.tensor_copy(stage[:, (b * 8 + sc) % 2, :],
                                          bkc2[0][1][:, :]), 1)
                cx.pp.release(bkc2[0][0])

            @blk.sync
            def _(e):
                cx.s_ev.wait(e)
                cx.s_st.inc(e.dma_start(ctxo[:, b, sc * 512:sc * 512 + 512],
                                        stage[:, (b * 8 + sc) % 2, :]), 16)


def E3_dst(w, h, sg):
    return w["E3h_l"][h][:, sg * 4:sg * 4 + 4, :]


TBLK = 256
NBLK = TOK // TBLK


def build_launch_C():
    import contextlib
    es = contextlib.ExitStack()
    nc = bass.Bass("TRN2", target_bir_lowering=False)
    xt = nc.dram_tensor("xt", [8, 128, TOK], F32, kind="ExternalInput").ap()
    ctxa = nc.dram_tensor("ctxa", [8, 128, TOK], BF16, kind="ExternalInput").ap()
    wy = nc.dram_tensor("wy", [32, 4, 32, 32], F32, kind="ExternalInput").ap()
    wz = nc.dram_tensor("wz", [128, 32, 128], F32, kind="ExternalInput").ap()
    wcs = nc.dram_tensor("wcs", [128, 128, 1024], F32, kind="ExternalInput").ap()
    wqT = nc.dram_tensor("wqT", [8, 128, 1024], F32, kind="ExternalInput").ap()
    wkT = nc.dram_tensor("wkT", [8, 128, 1024], F32, kind="ExternalInput").ap()
    wvT = nc.dram_tensor("wvT", [8, 128, 1024], F32, kind="ExternalInput").ap()
    xp = nc.dram_tensor("xp", [8, 128, TOK], F32, kind="ExternalOutput").ap()
    qbs = nc.dram_tensor("qbs", [16, 128, 2048], BF16, kind="ExternalOutput").ap()
    kbs = nc.dram_tensor("kbs", [16, 128, 2048], BF16, kind="ExternalOutput").ap()
    k1bs = nc.dram_tensor("k1bs", [16, 128, 2048], BF16, kind="ExternalOutput").ap()
    vbs = nc.dram_tensor("vbs", [16, 128, 1024], BF16, kind="ExternalOutput").ap()
    wcsb = nc.dram_tensor("wcsb", [128, 128, 1024], BF16).ap()
    wqb_s = nc.dram_tensor("wqb_s", [8, 128, 1024], BF16).ap()
    wkb_s = nc.dram_tensor("wkb_s", [8, 128, 1024], BF16).ap()
    wvb_s = nc.dram_tensor("wvb_s", [8, 128, 1024], BF16).ap()

    with es:
        cx = Ctx(es, nc, f32_elems=6000, bf16_elems=80000, n_psum=6)
        mask8 = es.enter_context(nc.sbuf_tensor("mask8", [128, 64], mybir.dt.uint8))
        xb = cx.f32.tile(8 * TBLK).rearrange("p (c t) -> p c t", c=8)
        mxt = cx.f32.tile(32)
        mnt = cx.f32.tile(32)
        cht = cx.f32.tile(32)
        sg = cx.f32.tile(32)
        xbf = cx.bf.tile(8 * TBLK).rearrange("p (c t) -> p c t", c=8)
        xn = cx.bf.tile(8 * TBLK).rearrange("p (c t) -> p c t", c=8)
        hbuf = cx.bf.tile(128 * TBLK).rearrange("p (z t) -> p z t", z=128)
        ytile = cx.bf.tile(2 * TBLK).rearrange("p (s t) -> p s t", s=2)
        dbuf = cx.bf.tile(2 * 4 * 1024).rearrange("p (s k e) -> p s k e", s=2, k=4)
        wproj = cx.bf.tile(8 * 1024).rearrange("p (c e) -> p c e", c=8)
        qkv = cx.bf.tile(3 * 2 * 1024).rearrange("p (w u e) -> p w u e", w=3, u=2)
        pm = cx.bf.tile(2 * 2048).rearrange("p (u e) -> p u e", u=2)
        msk = cx.bf.tile(2 * 16 * 64).rearrange("p (u h d) -> p u h d", u=2, h=16)
        wyb = cx.bf.tile(32 * 4 * 32).rearrange("p (y i c) -> p y i c", y=32, i=4)
        wzb = cx.bf.tile(32 * 128).rearrange("p (z c) -> p z c", z=32)

        with nc.Block() as blk:
            @blk.gpsimd
            def _(e):
                for i in range(8):
                    cx.s_ld.inc(e.dma_start(wcsb[i * 16:(i + 1) * 16],
                                            wcs[i * 16:(i + 1) * 16]), 16)
                cx.s_ld.inc(e.dma_start(wqb_s, wqT), 16)
                cx.s_ld.inc(e.dma_start(wkb_s, wkT), 16)
                cx.s_ld.inc(e.dma_start(wvb_s, wvT), 16)
                for i in range(4):
                    cx.s_ld.inc(e.dma_start(wyb[32 * i:32 * i + 32, :, i, :],
                                            wy[:, i].rearrange("y p c -> p y c")), 16)
                    cx.s_ld.inc(e.dma_start(
                        wzb[32 * i:32 * i + 32, :, :],
                        wz.rearrange("(g z) p c -> z g p c", z=4)[i]
                        .rearrange("g p c -> p g c")), 16)
            for eng in ["sync", "vector", "scalar", "tensor", "gpsimd"]:
                def _w(e):
                    cx.s_ld.wait(e)
                getattr(blk, eng)(_w)

        for tb in range(NBLK):
            emit_C_block(cx, tb, xt, ctxa, xb, xbf, xn, hbuf, ytile, dbuf, wproj,
                         qkv, pm, msk, mask8, mxt, mnt, cht, sg, wyb, wzb,
                         wcsb, wqb_s, wkb_s, wvb_s, xp, qbs, kbs, k1bs, vbs)

        with nc.Block() as blk:
            @blk.sync
            def _(e):
                cx.s_st.wait(e)
    return nc


def emit_C_block(cx, tb, xt, ctxa, xb, xbf, xn, hbuf, ytile, dbuf, wproj, qkv,
                 pm, msk, mask8, mxt, mnt, cht, sg, wyb, wzb,
                 wcsb, wqb_s, wkb_s, wvb_s, xp, qbs, kbs, k1bs, vbs):
    nc = cx.nc
    lo = tb * TBLK

    with nc.Block() as blk:
        @blk.sync
        def _(e):
            cx.s_ld.inc(e.dma_start(xb, xt.rearrange("c p t -> p c t")[:, :, lo:lo + TBLK]), 16)
            cx.s_ld.inc(e.dma_start(xbf, ctxa.rearrange("c p t -> p c t")[:, :, lo:lo + TBLK]), 16)
        @blk.vector
        def _(e):
            cx.s_ld.wait(e)
            for dc in range(8):
                e.tensor_tensor(xb[:, dc, :], xb[:, dc, :], xbf[:, dc, :], op=OP.add)
            for dc in range(8):
                op = e.tensor_copy(xbf[:, dc, :], xb[:, dc, :])
            cx.s_ev.inc(op, 1)

    emit_ln(cx, xbf, xn, TBLK)

    emit_ffn_block(cx, xb, xbf, xn, hbuf, ytile, dbuf, wyb, wzb, wcsb, xp, lo)

    emit_ln(cx, xbf, xn, TBLK)

    for w_i, (wsrc, scale) in enumerate([(wqb_s, SCALE_B), (wkb_s, None), (wvb_s, None)]):
        with nc.Block() as blk:
            @blk.sync
            def _(e, wsrc=wsrc):
                cx.s_ld.inc(e.dma_start(wproj, wsrc.rearrange("c p e -> p c e")), 16)
            @blk.tensor
            def _(e):
                cx.s_ld.wait(e)
        for tt in range(2):
            for eo in range(2):
                with nc.Block() as blk:
                    bk = [None]

                    @blk.tensor
                    def _(e, w_i=w_i, tt=tt, eo=eo):
                        bk[0] = cx.pp.acquire(e)
                        for ci in range(8):
                            mm = e.matmul(bk[0][1][:, :],
                                          xn[:, ci, tt * 128:tt * 128 + 128],
                                          wproj[:, ci, eo * 512:eo * 512 + 512],
                                          start=(ci == 0), stop=(ci == 7))
                        cx.s_mm.inc(mm, 1)

                    @blk.vector
                    def _(e, w_i=w_i, tt=tt, eo=eo, scale=scale):
                        cx.s_mm.wait(e)
                        dst = qkv[:, w_i, tt, eo * 512:eo * 512 + 512]
                        if scale is not None:
                            op = e.tensor_scalar(dst, bk[0][1][:, :], scale, None,
                                                 op0=OP.mult)
                        else:
                            op = e.tensor_copy(dst, bk[0][1][:, :])
                        cx.s_ev.inc(op, 1)
                        cx.pp.release(bk[0][0])

    for w_i in range(3):
        tview = qkv[:, w_i, :, :].rearrange("p u (h d) -> p u h d", h=16)
        mxv = mxt.rearrange("p (u h) -> p u h", u=2)
        mnv = mnt.rearrange("p (u h) -> p u h", u=2)
        with nc.Block() as blk:
            @blk.vector
            def _(e, w_i=w_i, tview=tview, mxv=mxv, mnv=mnv):
                e.tensor_reduce(mxv, tview, axis=AX.X, op=OP.max)
                cx.s_ev.inc(e.tensor_reduce(mnv, tview, axis=AX.X, op=OP.min), 1)
                cx.s_ev.wait(e)
                cx.s_ev.inc(e.tensor_tensor(sg[:, 0:32], mxt[:, 0:32], mnt[:, 0:32],
                                            op=OP.add), 1)
                cx.s_ev.wait(e)
                cx.s_ev.inc(e.tensor_scalar(mask8[:, 0:32], sg[:, 0:32], 0.0, None,
                                            op0=OP.is_gt), 1)
                cx.s_ev.wait(e)
                cx.s_ev.inc(e.select(cht[:, 0:32], mask8[:, 0:32], mxt[:, 0:32],
                                     mnt[:, 0:32]), 1)
                cx.s_ev.wait(e)
                chb = cht[:, 0:32].rearrange("p (u h) -> p u h", u=2)
                e.tensor_tensor(msk, tview, _bc4(chb), op=OP.is_equal)
                cx.s_ev.inc(e.tensor_tensor(tview, msk, tview, op=OP.mult), 1)

    emit_C_outputs(cx, tb, qkv, pm, qbs, kbs, k1bs, vbs)


def emit_ffn_block(cx, xb, xbf, xn, hbuf, ytile, dbuf, wyb, wzb, wcsb, xp, lo):
    nc = cx.nc
    for yc in range(32):
        with nc.Block() as blk:
            bky = [None]
            bkz = [None]

            @blk.tensor
            def _(e, yc=yc):
                bky[0] = cx.pp.acquire(e)
                for i in range(4):
                    mm = e.matmul(bky[0][1][32 * i:32 * i + 32, 0:TBLK],
                                  wyb[32 * i:32 * i + 32, yc, i, :],
                                  xn[32 * i:32 * i + 32, yc % 8, :],
                                  start=True, stop=True, tile_position=(32 * i, 32 * i))
                cx.s_mm.inc(mm, 1)

            @blk.vector
            def _(e, yc=yc):
                cx.s_mm.wait(e)
                cx.s_ev.inc(e.tensor_copy(ytile[:, yc % 2, :], bky[0][1][:, 0:TBLK]), 1)
                cx.pp.release(bky[0][0])

            for zi in range(4):
                zc = 32 * (yc // 8) + 4 * (yc % 8) + zi

                @blk.tensor
                def _(e, yc=yc, zi=zi, zc=zc):
                    bkz[0] = cx.pp.acquire(e)
                    cx.s_ev.wait(e)
                    cx.s_mm.inc(e.matmul(bkz[0][1][:, 0:TBLK],
                                         wzb[32 * zi:32 * zi + 32, zc // 4, :],
                                         ytile[32 * zi:32 * zi + 32, yc % 2, :],
                                         start=True, stop=True,
                                         tile_position=(32 * zi, 0)), 1)

                @blk.scalar
                def _(e, zc=zc):
                    cx.s_mm.wait(e)
                    cx.s_sc.inc(e.activation(hbuf[:, zc, :], bkz[0][1][:, 0:TBLK],
                                             AF.Gelu), 1)

                @blk.vector
                def _(e):
                    cx.s_sc.wait(e)
                    cx.s_ev.inc(e.engine_nop(), 1)
                    cx.pp.release(bkz[0][0])

    with nc.Block() as blk:
        banks = [None]

        @blk.tensor
        def _(e):
            banks[0] = cx.pp.acquire(e, nbank=4)
        for g in range(32):
            @blk.sync
            def _(e, g=g):
                if g >= 2:
                    cx.s_mm.wait(e, cx.s_mm.v - 1)
                cx.s_ld.inc(e.dma_start(dbuf[:, g % 2, :, :],
                                        wcsb[g * 4:(g + 1) * 4].rearrange("k p e -> p k e")), 16)

            @blk.tensor
            def _(e, g=g):
                cx.s_ld.wait(e)
                for ki in range(4):
                    kc = g * 4 + ki
                    for ec in range(8):
                        mm = e.matmul(banks[0][ec // 2][1][:, (ec % 2) * TBLK:
                                                           (ec % 2) * TBLK + TBLK],
                                      dbuf[:, g % 2, ki, ec * 128:ec * 128 + 128],
                                      hbuf[:, kc, :],
                                      start=(kc == 0 and ec % 2 == 0), stop=(kc == 127))
                cx.s_mm.inc(mm, 1)

        @blk.vector
        def _(e):
            cx.s_mm.wait(e)
            for ec in range(8):
                e.tensor_tensor(xb[:, ec, :], xb[:, ec, :],
                                banks[0][ec // 2][1][:, (ec % 2) * TBLK:
                                                     (ec % 2) * TBLK + TBLK],
                                op=OP.add)
            cx.s_ev.inc(e.engine_nop(), 1)
            for ec in range(4):
                cx.pp.release(banks[0][ec][0])
            for dc in range(8):
                op = e.tensor_copy(xbf[:, dc, :], xb[:, dc, :])
            cx.s_ev.inc(op, 1)

    if xp is not None:
        with nc.Block() as blk:
            @blk.sync
            def _(e):
                cx.s_st.inc(e.dma_start(xp.rearrange("c p t -> p c t")[:, :, lo:lo + TBLK],
                                        xb), 16)


def _bc4(chb):
    return chb.to_broadcast(list(chb.shape) + [64])


def emit_C_outputs(cx, tb, qkv, pm, qbs, kbs, k1bs, vbs):
    nc = cx.nc
    with nc.Block() as blk:
        @blk.sync
        def _(e):
            for u in range(2):
                cx.s_st.inc(e.dma_start(vbs[2 * tb + u], qkv[:, 2, u, :]), 16)
    specs = [(0, "pm"), (1, "pm"), (1, "unit")]
    dsts = [qbs, kbs, k1bs]
    for si, (w_i, mode) in enumerate(specs):
        with nc.Block() as blk:
            @blk.vector
            def _(e, w_i=w_i, mode=mode):
                cx.s_st.wait(e)
                src = qkv[:, w_i, :, :].rearrange("p u (h d) -> p u h d", h=16)
                dv = pm.rearrange("p u (h d) -> p u h d", h=16, d=128)
                if mode == "pm":
                    e.tensor_scalar(dv[:, :, :, 0:64], src, 0.0, None, op0=OP.max)
                    op = e.tensor_scalar(dv[:, :, :, 64:128], src, 0.0, None, op0=OP.min)
                else:
                    e.tensor_scalar(dv[:, :, :, 0:64], src, 0.0, None, op0=OP.is_gt)
                    op = e.tensor_scalar(dv[:, :, :, 64:128], src, 0.0, None, op0=OP.is_lt)
                cx.s_ev.inc(op, 1)

            @blk.sync
            def _(e, si=si):
                cx.s_ev.wait(e)
                for u in range(2):
                    cx.s_st.inc(e.dma_start(dsts[si][2 * tb + u], pm[:, u, :]), 16)


def build_launch_E():
    import contextlib
    es = contextlib.ExitStack()
    nc = bass.Bass("TRN2", target_bir_lowering=False)
    xt = nc.dram_tensor("xt", [8, 128, TOK], F32, kind="ExternalInput").ap()
    ctxe = nc.dram_tensor("ctxe", [8, 128, TOK], BF16, kind="ExternalInput").ap()
    dene = nc.dram_tensor("dene", [16, TOK], F32, kind="ExternalInput").ap()
    woT = nc.dram_tensor("woT", [8, 128, 1024], F32, kind="ExternalInput").ap()
    wy = nc.dram_tensor("wy", [32, 4, 32, 32], F32, kind="ExternalInput").ap()
    wz = nc.dram_tensor("wz", [128, 32, 128], F32, kind="ExternalInput").ap()
    wcs = nc.dram_tensor("wcs", [128, 128, 1024], F32, kind="ExternalInput").ap()
    xo = nc.dram_tensor("xo", [8, 128, TOK], F32, kind="ExternalOutput").ap()
    wcsb = nc.dram_tensor("wcsb", [128, 128, 1024], BF16).ap()

    with es:
        cx = Ctx(es, nc, f32_elems=6000, bf16_elems=80000, n_psum=6)
        xb = cx.f32.tile(8 * TBLK).rearrange("p (c t) -> p c t", c=8)
        den = cx.f32.tile(TBLK)
        rd = cx.f32.tile(TBLK)
        xbf = cx.bf.tile(8 * TBLK).rearrange("p (c t) -> p c t", c=8)
        xn = cx.bf.tile(8 * TBLK).rearrange("p (c t) -> p c t", c=8)
        ctxn = cx.bf.tile(8 * TBLK).rearrange("p (c t) -> p c t", c=8)
        hbuf = cx.bf.tile(128 * TBLK).rearrange("p (z t) -> p z t", z=128)
        ytile = cx.bf.tile(2 * TBLK).rearrange("p (s t) -> p s t", s=2)
        dbuf = cx.bf.tile(2 * 4 * 1024).rearrange("p (s k e) -> p s k e", s=2, k=4)
        wos = cx.bf.tile(8 * 1024).rearrange("p (c e) -> p c e", c=8)
        wyb = cx.bf.tile(32 * 4 * 32).rearrange("p (y i c) -> p y i c", y=32, i=4)
        wzb = cx.bf.tile(32 * 128).rearrange("p (z c) -> p z c", z=32)
        sel16 = cx.f32.tile(1024)

        with nc.Block() as blk:
            @blk.gpsimd
            def _(e):
                cx.s_ld.inc(e.iota(sel16[:, :], [[1, 16], [0, 64]],
                                   channel_multiplier=-1,
                                   allow_small_or_imprecise_dtypes=True), 1)
            @blk.vector
            def _(e):
                cx.s_ld.wait(e)
                e.tensor_scalar(sel16[:, :], sel16[:, :], 0.0, None, op0=OP.is_equal)
            @blk.gpsimd
            def _(e):
                for i in range(8):
                    cx.s_ld.inc(e.dma_start(wcsb[i * 16:(i + 1) * 16],
                                            wcs[i * 16:(i + 1) * 16]), 16)
                cx.s_ld.inc(e.dma_start(wos, woT.rearrange("c p e -> p c e")), 16)
                for i in range(4):
                    cx.s_ld.inc(e.dma_start(wyb[32 * i:32 * i + 32, :, i, :],
                                            wy[:, i].rearrange("y p c -> p y c")), 16)
                    cx.s_ld.inc(e.dma_start(
                        wzb[32 * i:32 * i + 32, :, :],
                        wz.rearrange("(g z) p c -> z g p c", z=4)[i]
                        .rearrange("g p c -> p g c")), 16)
            for eng in ["sync", "vector", "scalar", "tensor", "gpsimd"]:
                def _w(e):
                    cx.s_ld.wait(e)
                getattr(blk, eng)(_w)

        for tb in range(NBLK):
            emit_E_block(cx, tb, xt, ctxe, dene, xb, den, rd, xbf, xn, ctxn,
                         hbuf, ytile, dbuf, wos, wyb, wzb, wcsb, xo, sel16)

        with nc.Block() as blk:
            @blk.sync
            def _(e):
                cx.s_st.wait(e)
    return nc


def emit_E_block(cx, tb, xt, ctxe, dene, xb, den, rd, xbf, xn, ctxn,
                 hbuf, ytile, dbuf, wos, wyb, wzb, wcsb, xo, sel16):
    nc = cx.nc
    lo = tb * TBLK

    with nc.Block() as blk:
        @blk.sync
        def _(e):
            cx.s_ld.inc(e.dma_start(xb, xt.rearrange("c p t -> p c t")[:, :, lo:lo + TBLK]), 16)
            cx.s_ld.inc(e.dma_start(xbf, ctxe.rearrange("c p t -> p c t")[:, :, lo:lo + TBLK]), 16)
            cx.s_ld.inc(e.dma_start(den[0:16, :], dene[:, lo:lo + TBLK]), 16)
        @blk.vector
        def _(e):
            cx.s_ld.wait(e)
            cx.s_ev.inc(e.reciprocal(rd[0:16, :], den[0:16, :]), 1)

    for dc in range(8):
        with nc.Block() as blk:
            bk = [None]

            @blk.tensor
            def _(e, dc=dc):
                bk[0] = cx.pp.acquire(e)
                for hh in range(2):
                    h = 2 * dc + hh
                    mm = e.matmul(bk[0][1][64 * hh:64 * hh + 64, 0:TBLK],
                                  sel16[0:16, 64 * h:64 * h + 64], rd[0:16, :],
                                  start=True, stop=True, tile_position=(0, 64 * hh))
                cx.s_mm.inc(mm, 1)

            @blk.vector
            def _(e, dc=dc):
                cx.s_mm.wait(e)
                cx.s_ev.inc(e.tensor_tensor(ctxn[:, dc, :], xbf[:, dc, :],
                                            bk[0][1][:, 0:TBLK], op=OP.mult), 1)
                cx.pp.release(bk[0][0])

    for ec in range(8):
        with nc.Block() as blk:
            bk = [None]

            @blk.tensor
            def _(e, ec=ec):
                bk[0] = cx.pp.acquire(e)
                for ci in range(8):
                    mm = e.matmul(bk[0][1][:, 0:TBLK], wos[:, ci, ec * 128:ec * 128 + 128],
                                  ctxn[:, ci, :], start=(ci == 0), stop=(ci == 7))
                cx.s_mm.inc(mm, 1)

            @blk.vector
            def _(e, ec=ec):
                cx.s_mm.wait(e)
                cx.s_ev.inc(e.tensor_tensor(xb[:, ec, :], xb[:, ec, :],
                                            bk[0][1][:, 0:TBLK], op=OP.add), 1)
                cx.pp.release(bk[0][0])
    with nc.Block() as blk:
        @blk.vector
        def _(e):
            for dc in range(8):
                op = e.tensor_copy(xbf[:, dc, :], xb[:, dc, :])
            cx.s_ev.inc(op, 1)

    emit_ln(cx, xbf, xn, TBLK)
    emit_ffn_block(cx, xb, xbf, xn, hbuf, ytile, dbuf, wyb, wzb, wcsb, None, lo)

    with nc.Block() as blk:
        @blk.sync
        def _(e):
            cx.s_st.inc(e.dma_start(xo.rearrange("c p t -> p c t")[:, :, lo:lo + TBLK], xb), 16)


def build_launch_D():
    import contextlib
    es = contextlib.ExitStack()
    nc = bass.Bass("TRN2", target_bir_lowering=False)
    qpm = nc.dram_tensor("qpm", [8, 128, 4096], BF16, kind="ExternalInput").ap()
    kpm = nc.dram_tensor("kpm", [8, 128, 4096], BF16, kind="ExternalInput").ap()
    ksw = nc.dram_tensor("ksw", [8, 128, 4096], BF16, kind="ExternalInput").ap()
    k1t = nc.dram_tensor("k1t", [8, 128, 32, 128], BF16, kind="ExternalInput").ap()
    vtk = nc.dram_tensor("vtk", [8, 128, 32, 64], BF16, kind="ExternalInput").ap()
    mc = nc.dram_tensor("mc", [128, 4, 512], F32, kind="ExternalInput").ap()
    lcn = nc.dram_tensor("lcn", [128, 4, 512], F32, kind="ExternalInput").ap()
    ltr = nc.dram_tensor("ltr", [128, 128], F32, kind="ExternalInput").ap()
    idn = nc.dram_tensor("idn", [128, 128], F32, kind="ExternalInput").ap()
    ctxb = nc.dram_tensor("ctxb", [128, 4, 4096], BF16, kind="ExternalOutput").ap()
    denb = nc.dram_tensor("denb", [2, 4, 4096], F32, kind="ExternalOutput").ap()

    with es:
        cx = Ctx(es, nc, f32_elems=3000, bf16_elems=62000, n_psum=6)
        bnkO = BankGuard(es, nc, "bnkO", [128, 512], F32, cx.s_ev)
        bnkP = None
        t = {}
        t["qpm"] = cx.bf.tile(2 * 4096).rearrange("p (s x) -> p s x", s=2)
        t["kpm"] = cx.bf.tile(2 * 4096).rearrange("p (s x) -> p s x", s=2)
        t["ksw"] = cx.bf.tile(2 * 4096).rearrange("p (s x) -> p s x", s=2)
        t["k1t"] = cx.bf.tile(2 * 32 * 128).rearrange("p (s u x) -> p s u x", s=2, u=32)
        t["vraw"] = cx.bf.tile(2 * 32 * 64).rearrange("p (s u x) -> p s u x", s=2, u=32)
        t["q1"] = cx.bf.tile(4096)
        t["vext"] = cx.bf.tile(32 * 65).rearrange("p (u x) -> p u x", u=32)
        t["nvext"] = cx.bf.tile(32 * 65).rearrange("p (u x) -> p u x", u=32)
        t["ebuf"] = cx.bf.tile(4 * 512).rearrange("p (d x) -> p d x", d=4)
        t["ubuf"] = cx.bf.tile(4 * 512).rearrange("p (d x) -> p d x", d=4)
        t["zbuf"] = cx.bf.tile(512)
        t["zw"] = cx.bf.tile(512)
        t["wbuf"] = cx.bf.tile(2 * 512).rearrange("p (s x) -> p s x", s=2)
        t["ptb"] = cx.bf.tile(512)
        t["early"] = cx.bf.tile(64)
        t["state_bf"] = cx.bf.tile(65)
        t["nd"] = cx.bf.tile(4096)
        t["mc"] = cx.bf.tile(4 * 512).rearrange("p (d x) -> p d x", d=4)
        t["lcn"] = cx.bf.tile(4 * 512).rearrange("p (d x) -> p d x", d=4)
        t["ltr"] = cx.bf.tile(128)
        t["idn"] = cx.bf.tile(128)
        t["state"] = cx.f32.tile(65)
        t["corr"] = cx.f32.tile(1)
        t["pt1"] = cx.f32.tile(512)

        with nc.Block() as blk:
            @blk.gpsimd
            def _(e):
                cx.s_ld.inc(e.dma_start(t["mc"], mc.rearrange("p d x -> p d x")), 16)
                cx.s_ld.inc(e.dma_start(t["lcn"], lcn.rearrange("p d x -> p d x")), 16)
                cx.s_ld.inc(e.dma_start(t["ltr"], ltr), 16)
                cx.s_ld.inc(e.dma_start(t["idn"], idn), 16)
            @blk.sync
            def _(e):
                for nm, src in [("qpm", qpm), ("kpm", kpm), ("ksw", ksw)]:
                    cx.s_ld.inc(e.dma_start(t[nm][:, 0, :], src[0]), 16)
                cx.s_ld.inc(e.dma_start(t["k1t"][:, 0, :, :], k1t[0]), 16)
                cx.s_ld.inc(e.dma_start(t["vraw"][:, 0, :, :], vtk[0]), 16)
            for eng in ["sync", "vector", "scalar", "tensor", "gpsimd"]:
                def _w(e):
                    cx.s_ld.wait(e)
                getattr(blk, eng)(_w)

        for bh in range(8):
            emit_attn_b_bh(cx, bh, t, bnkO, bnkP, qpm, kpm, ksw, k1t, vtk,
                           ctxb, denb)

        with nc.Block() as blk:
            @blk.sync
            def _(e):
                cx.s_st.wait(e)
    return nc


def emit_attn_b_bh(cx, bh, t, bnkO, bnkP, qpm, kpm, ksw, k1t, vtk, ctxb, denb):
    nc = cx.nc
    sl = bh % 2
    hl, b = bh // 4, bh % 4

    with nc.Block() as blk:
        @blk.sync
        def _(e):
            if bh + 1 < 8:
                for nm, src in [("qpm", qpm), ("kpm", kpm), ("ksw", ksw)]:
                    cx.s_ld.inc(e.dma_start(t[nm][:, 1 - sl, :], src[bh + 1]), 16)
                cx.s_ld.inc(e.dma_start(t["k1t"][:, 1 - sl, :, :], k1t[bh + 1]), 16)
                cx.s_ld.inc(e.dma_start(t["vraw"][:, 1 - sl, :, :], vtk[bh + 1]), 16)
        @blk.vector
        def _(e):
            QPM = t["qpm"][:, sl, :]
            e.tensor_scalar(t["q1"][:, :], QPM, 0.0, None, op0=OP.not_equal)
            e.tensor_copy(t["vext"][:, :, 0:64], t["vraw"][:, sl, :, :])
            e.memset(t["vext"][:, :, 64:65], 1.0)
            e.memset(t["state"][:, :], 0.0)
            e.memset(t["state_bf"][:, :], 0.0)
            e.memset(t["corr"][:, :], 0.0)
            cx.s_ev.inc(e.engine_nop(), 1)
            cx.s_ev.wait(e)
            cx.s_ev.inc(e.tensor_scalar(t["nvext"][:, :, :], t["vext"][:, :, :],
                                        -1.0, None, op0=OP.mult), 1)

    for qc in range(8):
        emit_attn_b_qc(cx, bh, qc, t, bnkO, bnkP)

    with nc.Block() as blk:
        @blk.sync
        def _(e):
            cx.s_st.inc(e.dma_start(ctxb[64 * hl:64 * hl + 64, b, :], t["nd"][0:64, :]), 16)
        @blk.gpsimd
        def _(e):
            cx.s_st.inc(e.dma_start(denb[hl:hl + 1, b, :], t["nd"][64:65, :]), 16)


def emit_attn_b_qc(cx, bh, qc, t, bnkO, bnkP):
    nc = cx.nc
    sl = bh % 2
    lo = qc * 512
    QPM = t["qpm"][:, sl, :]
    KPM = t["kpm"][:, sl, :]
    KSW = t["ksw"][:, sl, :]
    K1T = t["k1t"][:, sl, :, :]
    marks = {}

    with nc.Block() as blk:
        banks = {}

        @blk.tensor
        def _(e):
            banks["O"] = bnkO.acquire(e)
            e.matmul(banks["O"][0:65, :], t["state_bf"][:, :], t["q1"][:, lo:lo + 512],
                     start=True, stop=False)
            for dpos in range(4):
                kt = 4 * qc + dpos
                bk = cx.pp.acquire(e)
                banks[("d", dpos)] = bk
                e.matmul(bk[1][:, :], KPM[:, kt * 128:kt * 128 + 128], QPM[:, lo:lo + 512],
                         start=True, stop=False)
                mm = e.matmul(bk[1][:, :], t["idn"][:, :], t["mc"][:, dpos, :],
                              start=False, stop=True)
                cx.s_mm.inc(mm, 1)
                marks[("d", dpos)] = cx.s_mm.v
            bz = cx.pp.acquire(e)
            banks["z"] = bz
            e.matmul(bz[1][:, :], KPM[:, 0:128], QPM[:, lo:lo + 512],
                     start=True, stop=False)
            if qc == 0:
                e.matmul(bz[1][:, :], KSW[:, 0:128], QPM[:, lo:lo + 512],
                         start=False, stop=False)
                mm = e.matmul(bz[1][:, :], t["idn"][:, :], t["mc"][:, 0, :],
                              start=False, stop=True)
            else:
                mm = e.matmul(bz[1][:, :], KSW[:, 0:128], QPM[:, lo:lo + 512],
                              start=False, stop=True)
            cx.s_mm.inc(mm, 1)
            marks["z"] = cx.s_mm.v

        @blk.scalar
        def _(e):
            for dpos in range(4):
                e.wait_ge(cx.s_mm.h, marks[("d", dpos)])
                cx.s_sc.inc(e.activation(t["ebuf"][:, dpos, :],
                                         banks[("d", dpos)][1][:, :], AF.Exp), 1)
                marks[("e", dpos)] = cx.s_sc.v
            if qc == 0:
                e.wait_ge(cx.s_mm.h, marks["z"])
                cx.s_sc.inc(e.activation(t["early"][0:64, :],
                                         banks["z"][1][0:64, 0:64], AF.Exp), 1)
            marks["exps"] = cx.s_sc.v

        @blk.vector
        def _(e):
            for dpos in range(4):
                e.wait_ge(cx.s_sc.h, marks[("e", dpos)])
                cx.s_ev.inc(e.tensor_scalar(t["ubuf"][:, dpos, :],
                                            banks[("d", dpos)][1][:, :], 0.0, None,
                                            op0=OP.is_gt), 1)
            e.wait_ge(cx.s_sc.h, marks["exps"])
            cx.s_ev.inc(e.tensor_scalar(t["zbuf"][:, :], banks["z"][1][:, :],
                                        0.0, None, op0=OP.is_equal), 1)
            marks["uz"] = cx.s_ev.v

        @blk.tensor
        def _(e):
            e.wait_ge(cx.s_ev.h, marks["uz"])
            bp = cx.pp.acquire(e)
            banks["P"] = bp
            for dpos in range(4):
                mm = e.matmul(bp[1][0:1, :], cx.ones_col_bf[:, :],
                              t["ubuf"][:, dpos, :], start=(dpos == 0), stop=(dpos == 3))
            cx.s_mm.inc(mm, 1)
            marks["P"] = cx.s_mm.v

        @blk.vector
        def _(e):
            e.wait_ge(cx.s_mm.h, marks["P"])
            e.tensor_copy(t["pt1"][0:1, :], banks["O"][64:65, :])
            cx.s_ev.inc(e.tensor_tensor(t["ptb"][0:1, :], t["pt1"][0:1, :],
                                        banks["P"][1][0:1, :], op=OP.add), 1)
            cx.pp.release(banks["P"][0])
            marks["pt"] = cx.s_ev.v

        @blk.tensor
        def _(e):
            bc = cx.pp.acquire(e)
            banks["c"] = bc
            e.matmul(bc[1][:, :], t["ltr"][:, :], t["zbuf"][:, :], start=True, stop=False)

        @blk.tensor
        def _(e):
            e.wait_ge(cx.s_ev.h, marks["pt"])
            cx.s_mm.inc(e.matmul(banks["c"][1][:, :], cx.ones_row_bf[0:1, :],
                                 t["ptb"][0:1, :], start=False, stop=True), 1)
            marks["cz"] = cx.s_mm.v

        @blk.vector
        def _(e):
            e.wait_ge(cx.s_mm.h, marks["cz"])
            cx.s_ev.inc(e.scalar_tensor_tensor(t["zw"][:, :], banks["c"][1][:, :],
                                               64.0, t["zbuf"][:, :],
                                               op0=OP.is_le, op1=OP.mult), 1)
            e.wait_ge(cx.s_sc.h, marks["exps"])
            cx.s_ev.inc(e.engine_nop(), 1)
            cx.pp.release(banks["c"][0])
            for dpos in range(4):
                cx.pp.release(banks[("d", dpos)][0])
            cx.pp.release(banks["z"][0])
            marks["zw"] = cx.s_ev.v

        @blk.tensor
        def _(e):
            e.wait_ge(cx.s_ev.h, marks["zw"])
            for dpos in range(4):
                kt = 4 * qc + dpos
                e.matmul(banks["O"][0:65, :], t["vext"][:, kt, :], t["ebuf"][:, dpos, :],
                         start=False, stop=False)
                e.matmul(banks["O"][0:65, :], t["vext"][:, kt, :], t["ubuf"][:, dpos, :],
                         start=False, stop=False)
                e.matmul(banks["O"][0:65, :], t["nvext"][:, kt, :], t["lcn"][:, dpos, :],
                         start=False, stop=False)
            mm = e.matmul(banks["O"][0:65, :], t["vext"][:, 0, :], t["zw"][:, :],
                          start=False, stop=(qc == 0))
            if qc == 0:
                cx.s_mm.inc(mm, 1)
            marks["att0"] = cx.s_mm.v

        for kt in range(4 * qc):
            @blk.tensor
            def _(e, kt=kt):
                bk = cx.pp.acquire(e)
                banks[("f", kt % 2)] = bk
                cx.s_mm.inc(e.matmul(bk[1][:, :], KPM[:, kt * 128:kt * 128 + 128],
                                     QPM[:, lo:lo + 512], start=True, stop=True), 1)
                marks[("fs", kt)] = cx.s_mm.v

            @blk.scalar
            def _(e, kt=kt):
                e.wait_ge(cx.s_mm.h, marks[("fs", kt)])
                if kt >= 2:
                    e.wait_ge(cx.s_mm.h, marks[("fa", kt - 2)])
                cx.s_sc.inc(e.activation(t["wbuf"][:, kt % 2, :],
                                         banks[("f", kt % 2)][1][:, :], AF.Exp), 1)
                marks[("fw", kt)] = cx.s_sc.v

            @blk.vector
            def _(e, kt=kt):
                e.wait_ge(cx.s_sc.h, marks[("fw", kt)])
                cx.s_ev.inc(e.engine_nop(), 1)
                cx.pp.release(banks[("f", kt % 2)][0])

            @blk.tensor
            def _(e, kt=kt, qc=qc):
                e.wait_ge(cx.s_sc.h, marks[("fw", kt)])
                mm = e.matmul(banks["O"][0:65, :], t["vext"][:, kt, :],
                              t["wbuf"][:, kt % 2, :],
                              start=False, stop=(kt == 4 * qc - 1))
                cx.s_mm.inc(mm, 1)
                marks[("fa", kt)] = cx.s_mm.v
        marks["attend"] = cx.s_mm.v

        @blk.vector
        def _(e):
            e.wait_ge(cx.s_mm.h, marks["attend"])
            cx.s_ev.inc(e.tensor_scalar(t["nd"][0:65, lo:lo + 512], banks["O"][0:65, :],
                                        t["corr"][0:65, 0:1], None, op0=OP.subtract), 1)
            bnkO.release()
            marks["ep"] = cx.s_ev.v

        if qc == 0:
            @blk.tensor
            def _(e):
                e.wait_ge(cx.s_ev.h, marks["ep"])
                bo2 = cx.pp.acquire(e)
                banks["o2"] = bo2
                cx.s_mm.inc(e.matmul(bo2[1][0:65, 0:64], t["vext"][0:64, 0, :],
                                     t["early"][0:64, :], start=True, stop=True), 1)
                marks["early"] = cx.s_mm.v

            @blk.vector
            def _(e):
                e.wait_ge(cx.s_mm.h, marks["early"])
                cx.s_ev.inc(e.tensor_copy(t["nd"][0:65, 0:64],
                                          banks["o2"][1][0:65, 0:64]), 1)
                cx.pp.release(banks["o2"][0])

        @blk.tensor
        def _(e):
            bs = cx.pp.acquire(e)
            banks["s"] = bs
            for i, kt in enumerate(range(4 * qc, 4 * qc + 4)):
                e.matmul(bs[1][0:128, 0:65], K1T[:, kt, :], t["vext"][:, kt, :],
                         start=(i == 0), stop=(i == 3))
            bcr = cx.pp.acquire(e)
            banks["cr"] = bcr
            e.wait_ge(cx.s_ev.h, marks["ep"])
            for i, kt in enumerate(range(4 * qc, 4 * qc + 4)):
                mm = e.matmul(bcr[1][0:65, 0:1], t["vext"][:, kt, :],
                              cx.ones_col_bf[:, :], start=(i == 0), stop=(i == 3))
            cx.s_mm.inc(mm, 1)
            marks["upd"] = cx.s_mm.v

        @blk.vector
        def _(e):
            e.wait_ge(cx.s_mm.h, marks["upd"])
            e.tensor_tensor(t["state"][:, :], t["state"][:, :], banks["s"][1][0:128, 0:65],
                            op=OP.add)
            e.tensor_tensor(t["corr"][0:65, 0:1], t["corr"][0:65, 0:1],
                            banks["cr"][1][0:65, 0:1], op=OP.add)
            cx.s_ev.inc(e.engine_nop(), 1)
            cx.s_ev.wait(e)
            cx.s_ev.inc(e.tensor_copy(t["state_bf"][:, :], t["state"][:, :]), 1)
            cx.pp.release(banks["s"][0])
            cx.pp.release(banks["cr"][0])


def host_prep_A(ins):
    x = np.asarray(ins["L0_attn_a_hidden_states"], np.float32)
    wq = np.ascontiguousarray(np.asarray(ins["L0_attn_a_Wq"], np.float32)
                              .reshape(8, 128, 1024))
    wk = np.ascontiguousarray(np.asarray(ins["L0_attn_a_Wk"], np.float32)
                              .reshape(8, 128, 1024))
    wv = np.ascontiguousarray(np.asarray(ins["L0_attn_a_Wv"], np.float32)
                              .reshape(8, 128, 1024))
    bq = np.ascontiguousarray(np.asarray(ins["L0_attn_a_bq"], np.float32)
                              .reshape(8, 128).T)
    bk = np.ascontiguousarray(np.asarray(ins["L0_attn_a_bk"], np.float32)
                              .reshape(8, 128).T)
    bv = np.asarray(ins["L0_attn_a_bv"], np.float32).reshape(1, 1024)
    maps = []
    for c in range(NCORES):
        b, h = c // 2, c % 2
        xs = x[b, h * TOK:(h + 1) * TOK, :]
        xtc = np.ascontiguousarray(xs.T).reshape(8, 128, TOK)
        maps.append(dict(xt=xtc, wq=wq, wk=wk, wv=wv, bq=bq, bk=bk, bv=bv))
    return maps


def host_prep_B(ins, outs_a):
    wq = np.asarray(ins["L0_attn_a_Wq"], np.float32)
    wk = np.asarray(ins["L0_attn_a_Wk"], np.float32)
    bq = np.asarray(ins["L0_attn_a_bq"], np.float32)
    bk = np.asarray(ins["L0_attn_a_bk"], np.float32)
    cw = np.asarray(ins["L0_attn_a_conv_w"], np.float32)[:, 0, :, 0]

    iconsts = np.zeros((128, 4, 128), np.float32)
    iconsts[:, 0, :] = np.eye(128)
    for i, a in enumerate([7.0, 15.0, 13.0]):
        blk = np.concatenate([np.eye(64), np.eye(64)], axis=1) * a
        iconsts[0:64, 1 + i, :] = blk

    maps = []
    for cc in range(NCORES):
        qam = np.stack([np.concatenate([np.asarray(outs_a[2 * b]["qa"])[cc],
                                        np.asarray(outs_a[2 * b + 1]["qa"])[cc]], axis=-1)
                        for b in range(4)], axis=1)
        kam = np.stack([np.concatenate([np.asarray(outs_a[2 * b]["ka"])[cc],
                                        np.asarray(outs_a[2 * b + 1]["ka"])[cc]], axis=-1)
                        for b in range(4)], axis=1)
        vam = np.zeros((128, 4, 32, 128), BFNP)
        for b in range(4):
            for st in range(32):
                src = np.asarray(outs_a[2 * b + st // 16]["va"])[st % 16]
                vam[:, b, st, :] = src[:, cc * 128:(cc + 1) * 128]
        xlg = np.zeros((128, 8, 4, 64), np.float32)
        for b in range(4):
            for half in range(2):
                xs = np.asarray(outs_a[2 * b + half]["xl"]).reshape(128, 8, 32)
                xlg[:, :, b, half * 32:(half + 1) * 32] = xs
        wqm = np.ascontiguousarray(
            wq[:, cc * 128:(cc + 1) * 128].reshape(8, 128, 128).transpose(1, 0, 2))
        wkm = np.ascontiguousarray(
            wk[:, cc * 128:(cc + 1) * 128].reshape(8, 128, 128).transpose(1, 0, 2))
        bqm = np.ascontiguousarray(bq[cc * 128:(cc + 1) * 128].reshape(128, 1))
        bkm = np.ascontiguousarray(bk[cc * 128:(cc + 1) * 128].reshape(128, 1))
        tb = np.zeros((128, 2, 3, 128), np.float32)
        for hl in range(2):
            wvec = cw[2 * cc + hl]
            for r in range(32):
                for so in range(128):
                    u = r - so
                    if 0 <= u < 65:
                        tb[96 + r, hl, 0, so] = wvec[u]
            for si in range(128):
                for so in range(128):
                    u = si - so + 32
                    if 0 <= u < 65:
                        tb[si, hl, 1, so] = wvec[u]
            for r in range(32):
                for so in range(128):
                    u = 160 + r - so
                    if 0 <= u < 65:
                        tb[r, hl, 2, so] = wvec[u]
        maps.append(dict(qa=qam.astype(BFNP), ka=kam.astype(BFNP), va=vam,
                         xlg=xlg, wqm=wqm, wkm=wkm, bqm=bqm, bkm=bkm,
                         tb=tb, iconsts=iconsts))
    return maps


def host_prep_C(ins, outs_b, maps_a):
    rm = np.asarray(ins["L0_ffn_rail_maker"], np.float32)
    ru = np.asarray(ins["L0_ffn_rail_up_proj"], np.float32)
    wc = np.asarray(ins["L0_ffn_w_cproj"], np.float32)
    wqb = np.asarray(ins["L0_attn_b_wq"], np.float32)
    wkb = np.asarray(ins["L0_attn_b_wk"], np.float32)
    wvb = np.asarray(ins["L0_attn_b_wv"], np.float32)

    wy = np.zeros((32, 4, 32, 32), np.float32)
    for yc in range(32):
        h = yc // 8
        for i in range(4):
            for r0 in range(2):
                rail = 8 * (yc % 8) + 2 * i + r0
                wy[yc, i, 16 * r0:16 * r0 + 16, 16 * r0:16 * r0 + 16] = rm[h, rail]
    wz = np.zeros((128, 32, 128), np.float32)
    for zc in range(128):
        h = zc // 32
        for r0 in range(2):
            rail = 2 * (zc % 32) + r0
            wz[zc, 16 * r0:16 * r0 + 16, 64 * r0:64 * r0 + 64] = ru[h, rail].T
    wcs = np.ascontiguousarray(wc.T.reshape(128, 128, 1024))
    wqT = np.ascontiguousarray(wqb.T.reshape(8, 128, 1024))
    wkT = np.ascontiguousarray(wkb.T.reshape(8, 128, 1024))
    wvT = np.ascontiguousarray(wvb.T.reshape(8, 128, 1024))

    maps = []
    for c in range(NCORES):
        b, half = c // 2, c % 2
        ctxa = np.zeros((8, 128, TOK), BFNP)
        for dc in range(8):
            ctxa[dc] = np.asarray(outs_b[dc]["ctxo"])[:, b, half * TOK:(half + 1) * TOK]
        maps.append(dict(xt=maps_a[c]["xt"], ctxa=ctxa, wy=wy, wz=wz, wcs=wcs,
                         wqT=wqT, wkT=wkT, wvT=wvT))
    return maps


def host_prep_D(outs_c):
    mc = np.zeros((128, 4, 512), np.float32)
    lcn = np.zeros((128, 4, 512), np.float32)
    for dpos in range(4):
        kj = np.arange(128)[:, None] + 128 * dpos
        qi = np.arange(512)[None, :]
        mc[:, dpos, :] = np.where(kj > qi, np.float32(-1e30), 0.0)
        lcn[:, dpos, :] = np.where(kj <= qi, np.float32(1.0), 0.0)
    ltr = (np.arange(128)[:, None] <= np.arange(128)[None, :]).astype(np.float32)
    idn = np.eye(128, dtype=np.float32)

    maps = []
    for cc in range(NCORES):
        qpm_d = np.zeros((8, 128, 4096), BFNP)
        kpm_d = np.zeros((8, 128, 4096), BFNP)
        ksw_d = np.zeros((8, 128, 4096), BFNP)
        k1t_d = np.zeros((8, 128, 32, 128), BFNP)
        vtk_d = np.zeros((8, 128, 32, 64), BFNP)
        for bh in range(8):
            hl, b = bh // 4, bh % 4
            hg = 2 * cc + hl
            for half in range(2):
                src = outs_c[2 * b + half]
                qt = np.asarray(src["qbs"])[:, :, hg * 128:(hg + 1) * 128]
                kt_ = np.asarray(src["kbs"])[:, :, hg * 128:(hg + 1) * 128]
                k1_ = np.asarray(src["k1bs"])[:, :, hg * 128:(hg + 1) * 128]
                vt_ = np.asarray(src["vbs"])[:, :, hg * 64:(hg + 1) * 64]
                tok0 = half * 2048
                qpm_d[bh, :, tok0:tok0 + 2048] = qt.reshape(2048, 128).T
                kpm_d[bh, :, tok0:tok0 + 2048] = kt_.reshape(2048, 128).T
                k1t_d[bh, :, half * 16:(half + 1) * 16, :] = \
                    k1_.transpose(1, 0, 2)
                vtk_d[bh, :, half * 16:(half + 1) * 16, :] = vt_.transpose(1, 0, 2)
            ksw_d[bh, 0:64, :] = kpm_d[bh, 64:128, :]
            ksw_d[bh, 64:128, :] = kpm_d[bh, 0:64, :]
        maps.append(dict(qpm=qpm_d, kpm=kpm_d, ksw=ksw_d, k1t=k1t_d, vtk=vtk_d,
                         mc=mc, lcn=lcn, ltr=ltr, idn=idn))
    return maps


def host_prep_E(ins, outs_c, outs_d, maps_c):
    wo = np.asarray(ins["L0_attn_b_wo"], np.float32)
    woT = np.ascontiguousarray(wo.T.reshape(8, 128, 1024))
    ctxg = np.concatenate([np.asarray(outs_d[cc]["ctxb"]) for cc in range(8)], axis=0)
    deng = np.concatenate([np.asarray(outs_d[cc]["denb"]) for cc in range(8)], axis=0)
    deng2 = np.zeros((16, 4, 4096), np.float32)
    for cc in range(8):
        deng2[2 * cc:2 * cc + 2] = np.asarray(outs_d[cc]["denb"])
    maps = []
    for c in range(NCORES):
        b, half = c // 2, c % 2
        tok = slice(half * TOK, (half + 1) * TOK)
        ctxe = np.ascontiguousarray(ctxg[:, b, tok].reshape(8, 128, TOK))
        dene = np.ascontiguousarray(deng2[:, b, tok])
        maps.append(dict(xt=np.asarray(outs_c[c]["xp"]), ctxe=ctxe, dene=dene,
                         woT=woT, wy=maps_c[c]["wy"], wz=maps_c[c]["wz"],
                         wcs=maps_c[c]["wcs"]))
    return maps


_NC_CACHE = {}


def _get_nc(name, builder):
    if name not in _NC_CACHE:
        _NC_CACHE[name] = builder()
    return _NC_CACHE[name]


def kernel(**inputs):
    ins = inputs
    cores = list(range(NCORES))
    maps_a = host_prep_A(ins)
    res_a = run_bass_kernel_spmd(_get_nc("A", build_launch_A), maps_a, core_ids=cores)
    outs_a = [res_a.results[c] for c in cores]
    maps_b = host_prep_B(ins, outs_a)
    res_b = run_bass_kernel_spmd(_get_nc("B", build_launch_B), maps_b, core_ids=cores)
    outs_b = [res_b.results[c] for c in cores]
    maps_c = host_prep_C(ins, outs_b, maps_a)
    res_c = run_bass_kernel_spmd(_get_nc("C", build_launch_C), maps_c, core_ids=cores)
    outs_c = [res_c.results[c] for c in cores]
    maps_d = host_prep_D(outs_c)
    res_d = run_bass_kernel_spmd(_get_nc("D", build_launch_D), maps_d, core_ids=cores)
    outs_d = [res_d.results[c] for c in cores]
    maps_e = host_prep_E(ins, outs_c, outs_d, maps_c)
    res_e = run_bass_kernel_spmd(_get_nc("E", build_launch_E), maps_e, core_ids=cores)
    out = np.zeros((B, S, DM), np.float32)
    for c in cores:
        b, half = c // 2, c % 2
        xo = np.asarray(res_e.results[c]["xo"], np.float32).reshape(DM, TOK)
        out[b, half * TOK:(half + 1) * TOK, :] = xo.T
    return out
